# version 03
# baseline (speedup 1.0000x reference)
import numpy as np
import ml_dtypes
import concourse.bass as bass
import concourse.mybir as mybir
from concourse.bass_utils import run_bass_kernel_spmd

F32 = mybir.dt.float32
BF16 = mybir.dt.bfloat16
I32 = mybir.dt.int32
AF = mybir.ActivationFunctionType
ALU = mybir.AluOpType
AX = mybir.AxisListType

DM = 4096
PROJ = 8512
CONV = 4352
NH = 64
HD = 64
KC = 4
HID = 11008
T = 1024
ET = 128
TT = T + ET
NQ = 8
Q = 128
NKD = 32
NMZ = 32
NMT = 67
GRP = 64
EPS = 1e-5
NMX = 34
PASS_HT = [range(0, 29), range(29, 58), range(58, 86)]
bfdt = ml_dtypes.bfloat16
ENGS = ("sld", "gld", "pe", "dv", "ac", "po", "sp", "sp2", "tdq", "hsh", "out")


class Sems:
    def __init__(self, nc):
        self.nc = nc
        self.h = {}
        self.c = {}
        self.marks = {}

    def new(self, *names):
        for name in names:
            self.h[name] = self.nc.semaphore(name).__enter__()
            self.c[name] = 0

    def inc(self, ins, name, n=1):
        ins.then_inc(self.h[name], n)
        self.c[name] += n
        return self.c[name]

    def wait(self, eng, name, val=None):
        eng.wait_ge(self.h[name], self.c[name] if val is None else val)

    def mark(self, label, name):
        self.marks[label] = (name, self.c[name])

    def wmark(self, eng, label):
        name, v = self.marks[label]
        eng.wait_ge(self.h[name], v)

    def wmark_if(self, eng, label):
        if label in self.marks:
            self.wmark(eng, label)

    def barrier(self, engs):
        snap = [(n, self.c[n]) for n in ENGS]
        for e in engs:
            for n, v in snap:
                if v > 0:
                    e.wait_ge(self.h[n], v)


def build_nc(debug=(), stop_after=None):
    dbg = set(debug)
    nc = bass.Bass("TRN2", target_bir_lowering=False)

    x_pad = nc.dram_tensor("x_pad", [TT, DM], F32, kind="ExternalInput").ap()
    w_in = nc.dram_tensor("w_in", [DM, PROJ], F32, kind="ExternalInput").ap()
    conv_w = nc.dram_tensor("conv_w", [CONV, KC], F32, kind="ExternalInput").ap()
    a_log = nc.dram_tensor("a_log", [NH], F32, kind="ExternalInput").ap()
    dt_bias = nc.dram_tensor("dt_bias", [NH], F32, kind="ExternalInput").ap()
    d_vec = nc.dram_tensor("d_vec", [NH], F32, kind="ExternalInput").ap()
    norm_w = nc.dram_tensor("norm_w", [DM], F32, kind="ExternalInput").ap()
    ln_w = nc.dram_tensor("ln_w", [DM], F32, kind="ExternalInput").ap()
    mup = nc.dram_tensor("mup", [PROJ], F32, kind="ExternalInput").ap()
    w_out = nc.dram_tensor("w_out", [DM, DM], F32, kind="ExternalInput").ap()
    w1q = nc.dram_tensor("w1q", [HID * DM // GRP // 2, GRP], I32, kind="ExternalInput").ap()
    s1q = nc.dram_tensor("s1q", [HID * DM // GRP], F32, kind="ExternalInput").ap()
    w2q = nc.dram_tensor("w2q", [HID * DM // GRP // 2, GRP], I32, kind="ExternalInput").ap()
    s2q = nc.dram_tensor("s2q", [HID * DM // GRP], F32, kind="ExternalInput").ap()
    w3q = nc.dram_tensor("w3q", [HID * DM // GRP // 2, GRP], I32, kind="ExternalInput").ap()
    s3q = nc.dram_tensor("s3q", [HID * DM // GRP], F32, kind="ExternalInput").ap()
    flagv = nc.dram_tensor("flagv", [128, 1], F32, kind="ExternalInput").ap()
    nonce = nc.dram_tensor("nonce", [1, 16], I32, kind="ExternalInput").ap()
    og = nc.dram_tensor("og", [T, DM], F32, kind="ExternalOutput").ap()

    zT_sp = nc.dram_tensor("zT_sp", [NMZ, 128, T], BF16, kind="Internal").ap()
    xsT_sp = nc.dram_tensor("xsT_sp", [NMZ, 128, T], BF16, kind="Internal").ap()
    xnT_sp = nc.dram_tensor("xnT_sp", [NKD, 128, T], BF16, kind="Internal").ap()
    s_sp = nc.dram_tensor("s_sp", [NQ, 128, DM], BF16, kind="Internal").ap()
    g1_sp = nc.dram_tensor("g1_sp", [NQ, 128, DM], F32, kind="Internal").ap()
    g2_sp = nc.dram_tensor("g2_sp", [NQ, 128, DM], F32, kind="Internal").ap()
    shH = nc.dram_tensor("shH", [256, NH * HD], BF16, kind="Internal", addr_space="Shared").ap()
    shF = nc.dram_tensor("shF", [2, 16], I32, kind="Internal", addr_space="Shared").ap()

    id128b = nc.inline_tensor(np.eye(128, dtype=bfdt), "id128b_c").ap()
    id128f = nc.inline_tensor(np.eye(128, dtype=np.float32), "id128f_c").ap()
    id64f = nc.inline_tensor(np.eye(64, dtype=np.float32), "id64f_c").ap()
    mask_np = np.where(np.arange(Q)[None, :] >= np.arange(Q)[:, None], 0.0, -1e30).astype(np.float32)
    mask_c = nc.inline_tensor(mask_np, "mask_c").ap()

    dbg_aps = {}

    def dbg_t(name, shape, dt=F32):
        dbg_aps[name] = nc.dram_tensor("dbg_" + name, shape, dt, kind="ExternalOutput").ap()
        return dbg_aps[name]

    S = Sems(nc)
    S.new(*ENGS)
    S.new("sldx0", "sldx1", "spx0", "spx1", "gldx0", "gldx1", "sp2x0", "sp2x1", "pwx0", "pwx1", "xnld", "sbld", "ginld")
    _sal = {}

    def salloc(name, shape, dt, side="left"):
        cm = nc.sbuf_tensor(name, shape, dt, side=side)
        t = cm.__enter__()
        _sal[name] = cm
        return t

    def sfree(*names):
        for n in names:
            _sal.pop(n).__exit__(None, None, None)
    sy, gp, te, ve, sc = nc.sync, nc.gpsimd, nc.tensor, nc.vector, nc.scalar
    ALL_E = (sy, gp, te, ve, sc)

    p_ta = nc.alloc_psum_tensor("p_ta", [128, 1024], BF16)
    p_tb = nc.alloc_psum_tensor("p_tb", [128, 1024], BF16)
    P = [nc.alloc_psum_tensor("p_%d" % i, [128, 512], F32) for i in range(6)]

    ident_b = salloc("ident_b", [128, 128], BF16)
    ident_f = salloc("ident_f", [128, 128], F32)
    ident64 = salloc("ident64", [64, 64], F32)
    mask_sb = salloc("mask_sb", [128, 128], F32)
    lnw_sb = salloc("lnw_sb", [128, NKD], F32)
    mupc_sb = salloc("mupc_sb", [128, NMT], F32)
    cwf_sb = salloc("cwf_sb", [128, NMX * KC], F32)
    prm_sb = salloc("prm_sb", [64, 8], F32)
    dbc_sb = salloc("dbc_sb", [128, NH], F32)
    nwcol_sb = salloc("nwcol_sb", [128, NKD], F32)
    ones1f = salloc("ones1f", [1, 128], F32)
    onesq64 = salloc("onesq64", [64, Q], F32)
    flag_sb = salloc("flag_sb", [128, 1], F32)
    nonce_sb = salloc("nonce_sb", [1, 16], I32)
    pollbuf = salloc("pollbuf", [1, 16], I32)
    stat = salloc("stat", [128, 8], F32)

    lnw_t = salloc("lnw_t", [32, 128], F32)
    mup_t = salloc("mup_t", [67, 128], F32)
    nw_t = salloc("nw_t", [32, 128], F32)

    S.inc(sy.dma_start(out=ident_b[:], in_=id128b), "sld", 16)
    S.inc(sy.dma_start(out=ident_f[:], in_=id128f), "sld", 16)
    S.inc(sy.dma_start(out=ident64[:], in_=id64f), "sld", 16)
    S.inc(sy.dma_start(out=mask_sb[:], in_=mask_c), "sld", 16)
    S.inc(sy.dma_start(out=lnw_t[:], in_=ln_w.rearrange("(k p) -> k p", p=128)), "sld", 16)
    S.inc(ve.memset(mup_t[64:67, :], 0.0), "dv")
    S.wait(sy, "dv")
    S.inc(sy.dma_start(out=mup_t[0:66, :], in_=mup[0:66 * 128].rearrange("(m p) -> m p", p=128)), "sld", 16)
    S.inc(sy.dma_start(out=mup_t[66:67, 0:64], in_=mup[66 * 128:PROJ].rearrange("(m p) -> m p", p=64)), "sld", 16)
    S.inc(sy.dma_start(out=prm_sb[:, 0:1], in_=a_log.rearrange("(p o) -> p o", o=1)), "sld", 16)
    S.inc(sy.dma_start(out=prm_sb[:, 1:2], in_=dt_bias.rearrange("(p o) -> p o", o=1)), "sld", 16)
    S.inc(sy.dma_start(out=prm_sb[:, 2:3], in_=d_vec.rearrange("(p o) -> p o", o=1)), "sld", 16)
    S.inc(sy.dma_start(out=flag_sb[:], in_=flagv), "sld", 16)
    S.inc(sy.dma_start(out=nonce_sb[:], in_=nonce), "sld", 16)
    for i in range(4):
        mm0, mm1 = i * 9, min(NMX, i * 9 + 9)
        S.inc(sy.dma_start(out=cwf_sb[:, mm0 * KC:mm1 * KC].rearrange("p (m j) -> p m j", j=KC),
                           in_=conv_w[mm0 * 128:mm1 * 128, :].rearrange("(m p) j -> p m j", p=128)), "sld", 16)
    drow = salloc("drow", [1, 64], F32)
    S.inc(sy.dma_start(out=nw_t[:], in_=norm_w.rearrange("(k p) -> k p", p=128)), "sld", 16)
    S.mark("setup_ld", "sld")

    S.wmark(ve, "setup_ld")
    S.inc(ve.memset(ones1f[:], 1.0), "dv")
    S.wait(ve, "dv")
    S.inc(ve.memset(onesq64[:], 1.0), "dv")
    S.mark("ones", "dv")

    S.wmark(sc, "setup_ld")
    S.inc(sc.activation(prm_sb[:, 3:4], prm_sb[:, 0:1], AF.Exp), "ac")
    S.wait(sc, "ac")
    S.inc(sc.mul(prm_sb[:, 3:4], prm_sb[:, 3:4], -1.0), "ac")
    S.mark("A_ready", "ac")

    S.wmark(te, "A_ready")
    S.inc(te.transpose(P[0][0:1, 0:64], prm_sb[:, 2:3], ident64[:, :]), "pe")
    S.wait(ve, "pe")
    S.inc(ve.tensor_copy(drow[:], P[0][0:1, 0:64]), "dv")
    S.wait(te, "dv")
    S.wmark(te, "ones")
    S.inc(te.matmul(P[0][:, 0:64], ones1f[:, :], drow[:, :], start=True, stop=True), "pe")
    S.wait(ve, "pe")
    S.inc(ve.tensor_copy(dbc_sb[:], P[0][:, 0:64]), "dv")

    S.wmark(te, "setup_ld")
    S.wait(te, "dv")
    S.inc(te.transpose(P[1][0:128, 0:32], lnw_t[:, :], ident_f[0:32, 0:32]), "pe")
    S.mark("lnwT_ps", "pe")
    S.inc(te.transpose(P[2][0:128, 0:67], mup_t[:, :], ident_f[0:67, 0:67]), "pe")
    S.mark("mupT_ps", "pe")
    S.wmark(ve, "lnwT_ps")
    S.inc(ve.tensor_copy(lnw_sb[:, :], P[1][0:128, 0:32]), "dv")
    S.wmark(ve, "mupT_ps")
    S.inc(ve.tensor_copy(mupc_sb[:, :], P[2][0:128, 0:67]), "dv")
    S.wait(te, "dv")
    S.inc(te.transpose(P[1][0:128, 0:32], nw_t[:, :], ident_f[0:32, 0:32]), "pe")
    S.mark("nwT_ps", "pe")
    S.wmark(ve, "nwT_ps")
    S.inc(ve.tensor_copy(nwcol_sb[:, :], P[1][0:128, 0:32]), "dv")
    S.mark("lnw_mup", "dv")

    sfree("drow", "nw_t", "mup_t", "lnw_t")

    if stop_after == "setup":
        S.wait(sy, "sld")
        S.barrier(ALL_E)
        return nc, dbg_aps
    dts_sb = salloc("dts_sb", [64, T], F32)
    logdA_sb = salloc("logdA_sb", [64, T], F32)
    logdt_sb = salloc("logdt_sb", [64, T], F32)
    bmT_sb = salloc("bmT_sb", [128, T], BF16)
    cmT_sb = salloc("cmT_sb", [128, T], BF16)
    dtT_sb = salloc("dtT_sb", [64, TT], F32)
    rrb_sb = salloc("rrb_sb", [128, TT], F32)
    rmsxT = salloc("rmsxT", [128, NKD * TT], BF16)
    rrow = salloc("rrow", [1, 128], F32)
    xsq = salloc("xsq", [128, DM], F32)
    xtile = salloc("xtile", [128, DM], F32)
    xtile2 = salloc("xtile2", [128, DM], F32)
    xnb = salloc("xnb", [128, DM], BF16)
    xstg = [salloc("xstg%d" % i, [128, 128], BF16) for i in range(2)]

    import os as _os
    _n05 = int(_os.environ.get("PH05_TILES", "9"))
    for i in range(_n05):
        xt = xtile if i % 2 == 0 else xtile2
        S.wmark_if(sy, "xtdone_%d" % (i - 2))
        S.inc(sy.dma_start(out=xt[:], in_=x_pad[i * 128:(i + 1) * 128, :]), "sldx%d" % (i % 2), 16)
        S.wait(ve, "sldx%d" % (i % 2))
        S.wait(ve, "dv")
        S.wmark_if(ve, "rrbcast_%d" % (i - 1))
        S.inc(ve.tensor_reduce(stat[:, 0:1], xt[:], AX.X, ALU.add, negate=True), "dv")
        S.wait(ve, "dv")
        S.inc(ve.tensor_scalar(stat[:, 0:1], stat[:, 0:1], 1.0 / DM, None, ALU.mult), "dv")
        S.wait(ve, "dv")
        S.inc(ve.tensor_scalar(xt[:], xt[:], stat[:, 0:1], None, ALU.add), "dv")
        S.wait(sc, "dv")
        S.wmark_if(sc, "xsqdone_%d" % (i - 1))
        S.inc(sc.activation(xsq[:], xt[:], AF.Square, accum_out=stat[:, 1:2]), "ac")
        S.mark("xsqdone_%d" % i, "ac")
        S.wait(ve, "ac")
        S.inc(ve.tensor_scalar(stat[:, 1:2], stat[:, 1:2], 1.0 / DM, EPS, ALU.mult, ALU.add), "dv")
        S.wait(sc, "dv")
        S.inc(sc.activation(stat[:, 2:3], stat[:, 1:2], AF.Sqrt), "ac")
        S.wait(ve, "ac")
        S.inc(ve.reciprocal(stat[:, 3:4], stat[:, 2:3]), "dv")
        S.wait(ve, "dv")
        S.inc(ve.tensor_scalar(xnb[:], xt[:], stat[:, 3:4], None, ALU.mult), "dv")
        S.mark("xtdone_%d" % i, "dv")
        S.wait(ve, "dv")
        S.inc(ve.tensor_scalar(stat[:, 4:5], stat[:, 1:2], EPS, None, ALU.subtract), "dv")
        S.wait(ve, "dv")
        S.inc(ve.tensor_tensor(stat[:, 5:6], stat[:, 3:4], stat[:, 3:4], ALU.mult), "dv")
        S.wait(ve, "dv")
        S.inc(ve.tensor_tensor(stat[:, 4:5], stat[:, 4:5], stat[:, 5:6], ALU.mult), "dv")
        S.wait(ve, "dv")
        S.inc(ve.tensor_scalar(stat[:, 4:5], stat[:, 4:5], EPS, None, ALU.add), "dv")
        S.wait(sc, "dv")
        S.inc(sc.activation(stat[:, 5:6], stat[:, 4:5], AF.Sqrt), "ac")
        S.wait(ve, "ac")
        S.inc(ve.reciprocal(stat[:, 6:7], stat[:, 5:6]), "dv")
        S.wait(te, "dv")
        S.wmark_if(te, "rrbcast_%d" % (i - 1))
        S.inc(te.transpose(P[0][0:1, 0:128], stat[:, 6:7], ident_f[:, :]), "pe")
        S.wait(ve, "pe")
        S.inc(ve.tensor_copy(rrow[:], P[0][0:1, 0:128]), "dv")
        S.wait(te, "dv")
        S.inc(te.matmul(P[0][:, 0:128], ones1f[:, :], rrow[:, :], start=True, stop=True), "pe")
        S.wait(ve, "pe")
        S.inc(ve.tensor_copy(rrb_sb[:, i * 128:(i + 1) * 128], P[0][:, 0:128]), "dv")
        S.mark("rrbcast_%d" % i, "dv")
        for k in range(NKD):
            pt = p_ta if k % 2 == 0 else p_tb
            S.wmark_if(te, "ptf_%d" % (k % 2))
            S.wmark_if(te, "ptv_%d" % (k % 2))
            S.wait(te, "dv")
            S.inc(te.transpose(pt[:, 0:128], xnb[:, k * 128:(k + 1) * 128], ident_b[:, :]), "pe")
            if i >= 1:
                stg = xstg[k % 2]
                S.wmark_if(sc, "xstgf_%d" % (k % 2))
                S.wmark_if(sc, "ptv_%d" % (k % 2))
                S.wait(sc, "pe")
                S.inc(sc.copy(stg[:, :], pt[:, 0:128]), "ac")
                S.mark("ptf_%d" % (k % 2), "ac")
                S.wait(ve, "ac")
                S.inc(ve.tensor_scalar(rmsxT[:, k * TT + i * 128:k * TT + (i + 1) * 128], stg[:, :], lnw_sb[:, k:k + 1], None, ALU.mult), "dv")
                S.mark("ptv_%d" % (k % 2), "dv")
                S.wait(sy, "dv")
                S.inc(sy.dma_start(out=xnT_sp[k, :, (i - 1) * 128:i * 128], in_=stg[:, :]), "spx%d" % (k % 2), 16)
                S.mark("xstgf_%d" % (k % 2), "spx%d" % (k % 2))
            else:
                S.wait(ve, "pe")
                S.inc(ve.tensor_scalar(rmsxT[:, k * TT + i * 128:k * TT + (i + 1) * 128], pt[:, 0:128], lnw_sb[:, k:k + 1], None, ALU.mult), "dv")
                S.mark("ptv_%d" % (k % 2), "dv")
                S.mark("ptf_%d" % (k % 2), "dv")
    S.mark("ph05_done", "dv")

    if stop_after == "ph05":
        S.wait(sy, "sp")
        S.barrier(ALL_E)
        return nc, dbg_aps
    sfree("xstg1", "xstg0", "xnb", "xtile2", "xtile", "xsq", "rrow")
    S.barrier(ALL_E)

    wm = [salloc("wm%d" % i, [128, NKD * 128], BF16) for i in range(2)]
    xbcT = [salloc("xbcT%d" % i, [128, TT], F32) for i in range(2)]
    cacc = salloc("cacc", [128, T], F32)
    zstage = salloc("zstage", [128, T], BF16)
    xsstage = salloc("xsstage", [128, T], BF16)

    S.wmark(gp, "setup_ld")
    for m in range(NMT):
        wmb = wm[m % 2]
        is_z = m < NMZ
        is_dt = m == NMT - 1
        mm = m - NMZ
        ncols = 64 if is_dt else 128
        S.wmark_if(gp, "wmfree_%d" % (m % 2))
        for kk in range(4):
            S.inc(gp.dma_start(
                out=wmb[:, kk * 8 * ncols:(kk + 1) * 8 * ncols].rearrange("p (k c) -> p k c", k=8),
                in_=w_in[kk * 8 * 128:(kk + 1) * 8 * 128, m * 128:m * 128 + ncols].rearrange("(k p) c -> p k c", p=128)),
                "gldx%d" % (m % 2), 16)
        S.mark("wld_%d" % m, "gldx%d" % (m % 2))
        tcs = [(128, 512), (640, 512)] if is_z else [(0, 512), (512, 512), (1024, 128)]
        if m == 0:
            S.wmark(te, "ph05_done")
        for ci, (t0, tn) in enumerate(tcs):
            pj = P[ci % 3]
            S.wmark(te, "wld_%d" % m)
            S.wmark_if(te, "pjev_%d" % (ci % 3))
            for k in range(NKD):
                ins = te.matmul(pj[0:ncols, 0:tn],
                                wmb[:, k * ncols:(k + 1) * ncols],
                                rmsxT[:, k * TT + t0:k * TT + t0 + tn],
                                start=(k == 0), stop=(k == NKD - 1))
            S.inc(ins, "pe")
            S.mark("pjmm_%d" % (ci % 3), "pe")
            S.wmark(ve, "pjmm_%d" % (ci % 3))
            S.wait(ve, "dv")
            if is_z:
                S.wmark_if(ve, "zspill")
                S.inc(ve.scalar_tensor_tensor(zstage[:, t0 - ET:t0 - ET + tn], pj[0:128, 0:tn], mupc_sb[:, m:m + 1], rrb_sb[:, t0:t0 + tn], ALU.mult, ALU.mult), "dv")
            elif is_dt:
                S.inc(ve.scalar_tensor_tensor(dtT_sb[0:64, t0:t0 + tn], pj[0:64, 0:tn], mupc_sb[0:64, m:m + 1], rrb_sb[0:64, t0:t0 + tn], ALU.mult, ALU.mult), "dv")
            else:
                S.inc(ve.scalar_tensor_tensor(xbcT[m % 2][:, t0:t0 + tn], pj[0:128, 0:tn], mupc_sb[:, m:m + 1], rrb_sb[:, t0:t0 + tn], ALU.mult, ALU.mult), "dv")
            S.mark("pjev_%d" % (ci % 3), "dv")
        S.marks["wmfree_%d" % (m % 2)] = ("pe", S.c["pe"])

        if is_z:
            S.wait(sy, "dv")
            S.inc(sy.dma_start(out=zT_sp[m, :, :], in_=zstage[:, :]), "sp", 16)
            S.mark("zspill", "sp")
        elif not is_dt:
            xb = xbcT[m % 2]
            S.wait(ve, "dv")
            S.wait(ve, "ac")
            S.inc(ve.tensor_scalar(cacc[:], xb[:, 125:125 + T], cwf_sb[:, mm * KC:mm * KC + 1], None, ALU.mult), "dv")
            for j in range(1, KC):
                S.wait(ve, "dv")
                S.inc(ve.scalar_tensor_tensor(cacc[:], xb[:, 125 + j:125 + j + T], cwf_sb[:, mm * KC + j:mm * KC + j + 1], cacc[:], ALU.mult, ALU.add), "dv")
            S.wait(sc, "dv")
            if mm < 32:
                S.wmark_if(sc, "xsspill")
                S.inc(sc.activation(xsstage[:, :], cacc[:], AF.Silu), "ac")
                S.wait(sy, "ac")
                S.inc(sy.dma_start(out=xsT_sp[mm, :, :], in_=xsstage[:, :]), "sp", 16)
                S.mark("xsspill", "sp")
            elif mm == 32:
                S.inc(sc.activation(bmT_sb[:, :], cacc[:], AF.Silu), "ac")
            else:
                S.inc(sc.activation(cmT_sb[:, :], cacc[:], AF.Silu), "ac")

    S.wait(ve, "dv")
    S.inc(ve.tensor_scalar(dtT_sb[0:64, ET:TT], dtT_sb[0:64, ET:TT], prm_sb[:, 1:2], None, ALU.add), "dv")
    S.wait(sc, "dv")
    S.inc(sc.activation(logdA_sb[0:64, :], dtT_sb[0:64, ET:TT], AF.Abs), "ac")
    S.wait(sc, "ac")
    S.inc(sc.activation(logdt_sb[0:64, :], logdA_sb[0:64, :], AF.Exp, scale=-1.0), "ac")
    S.wait(sc, "ac")
    S.inc(sc.activation(logdt_sb[0:64, :], logdt_sb[0:64, :], AF.Ln, bias=1.0), "ac")
    S.wait(ve, "ac")
    S.inc(ve.tensor_scalar(logdA_sb[0:64, :], dtT_sb[0:64, ET:TT], 0.0, None, ALU.max), "dv")
    S.wait(ve, "dv")
    S.inc(ve.tensor_tensor(dts_sb[0:64, :], logdA_sb[0:64, :], logdt_sb[0:64, :], ALU.add), "dv")
    S.wmark(ve, "A_ready")
    S.wait(ve, "dv")
    S.inc(ve.tensor_scalar(logdA_sb[0:64, :], dts_sb[0:64, :], prm_sb[:, 3:4], None, ALU.mult), "dv")
    S.wait(sc, "dv")
    S.inc(sc.activation(logdt_sb[0:64, :], dts_sb[0:64, :], AF.Ln), "ac")
    S.mark("dt_done", "ac")
    S.mark("ph1_done", "dv")

    if "zT" in dbg:
        d = dbg_t("zT", [NMZ, 128, T], BF16)
        S.wait(sy, "sp")
        S.inc(sy.dma_start(out=d, in_=zT_sp), "sp", 16)
    if "xsT" in dbg:
        d = dbg_t("xsT", [NMZ, 128, T], BF16)
        S.wait(sy, "sp")
        S.inc(sy.dma_start(out=d, in_=xsT_sp), "sp", 16)
    if "xnT" in dbg:
        d = dbg_t("xnT", [NKD, 128, T], BF16)
        S.wait(sy, "sp")
        S.inc(sy.dma_start(out=d, in_=xnT_sp), "sp", 16)
    if "dts" in dbg:
        d = dbg_t("dts", [64, T])
        S.wmark(sy, "dt_done")
        S.wmark(sy, "ph1_done")
        S.inc(sy.dma_start(out=d, in_=dts_sb[0:64, :]), "sp", 16)
    if "bcT" in dbg:
        d = dbg_t("bcT", [256, T], BF16)
        S.wmark(sy, "ph1_done")
        S.wait(sy, "ac")
        S.inc(sy.dma_start(out=d[0:128, :], in_=bmT_sb[:, :]), "sp", 16)
        S.inc(sy.dma_start(out=d[128:256, :], in_=cmT_sb[:, :]), "sp", 16)

    sfree("xsstage", "zstage", "cacc", "xbcT1", "xbcT0", "wm1", "wm0", "rmsxT", "rrb_sb", "dtT_sb")
    S.barrier(ALL_E)

    if stop_after == "ph1":
        S.wait(sy, "sp")
        S.wait(sy, "ac")
        return nc, dbg_aps
    y_acc = salloc("y_acc", [128, NQ * DM], BF16)
    hfp = salloc("hfp", [128, NH * HD], F32)
    hbf = salloc("hbf", [128, NH * HD], BF16)
    hg = salloc("hg", [128, NH * HD], BF16)
    xstq = [salloc("xstq%d" % i, [128, 32 * Q], BF16) for i in range(2)]
    xs_nat = salloc("xs_nat", [128, DM], BF16)
    bnat = salloc("bnat", [128, 128], BF16)
    cumc = salloc("cumc", [64, Q], F32)
    cumT = salloc("cumT", [128, 64], F32)
    ldtT = salloc("ldtT", [128, 64], F32)
    gT = salloc("gT", [128, 64], F32)
    vT_all = salloc("vT_all", [128, NQ * 64], F32)
    ecq_all = salloc("ecq_all", [128, NQ * 64], F32)
    gamb = salloc("gamb", [128, 64], F32)
    sTq = salloc("sTq", [128, 64], F32)
    gts = salloc("gts", [128, 128], F32)
    me_sb = salloc("me_sb", [128, 128], F32)
    mt_sb = [salloc("mt_sb%d" % i, [128, 128], BF16) for i in range(2)]
    xsw = [salloc("xsw%d" % i, [128, 64], BF16) for i in range(2)]
    crow = salloc("crow", [1, 128], F32)

    S.wait(ve, "dv")
    S.inc(ve.memset(hfp[:], 0.0), "dv")
    S.wait(ve, "dv")
    S.inc(ve.memset(hbf[:], 0.0), "dv")
    S.mark("hbf_init", "dv")
    S.marks["hbf_-1"] = S.marks["hbf_init"]

    for q in range(NQ):
        qs = slice(q * Q, (q + 1) * Q)
        xq = xstq[q % 2]
        S.wmark_if(sy, "xstq_free_%d" % (q % 2))
        if q == 0:
            S.wait(sy, "sp")
        for a in range(4):
            S.inc(sy.dma_start(out=xq[:, a * 8 * Q:(a + 1) * 8 * Q].rearrange("p (m t) -> p m t", m=8),
                               in_=xsT_sp[a * 8:(a + 1) * 8, :, qs].rearrange("m p t -> p m t")), "sp2x%d" % (q % 2), 16)
        S.mark("xsld_%d" % q, "sp2x%d" % (q % 2))

        if q == 0:
            S.wmark(ve, "dt_done")
            S.wmark(ve, "ph1_done")
        S.wmark_if(ve, "cum_cons_%d" % (q - 1))
        S.wait(ve, "dv")
        S.inc(ve.tensor_tensor_scan(cumc[:, :], onesq64[:, :], logdA_sb[0:64, qs], 0.0, ALU.mult, ALU.add), "dv")
        S.mark("cum_%d" % q, "dv")
        S.wmark(te, "cum_%d" % q)
        S.wmark_if(te, "p0_free")
        S.inc(te.transpose(P[0][0:128, 0:64], cumc[:, :], ident64[:, :]), "pe")
        S.mark("cumT_ps", "pe")
        S.wmark_if(te, "p1_free")
        S.inc(te.transpose(P[1][0:128, 0:64], logdt_sb[0:64, qs], ident64[:, :]), "pe")
        S.mark("ldtT_ps", "pe")
        S.wmark(ve, "cumT_ps")
        S.wmark_if(ve, "cumT_cons_%d" % (q - 1))
        S.inc(ve.tensor_copy(cumT[:, :], P[0][0:128, 0:64]), "dv")
        S.mark("cumT_%d" % q, "dv")
        S.wmark(ve, "ldtT_ps")
        S.inc(ve.tensor_copy(ldtT[:, :], P[1][0:128, 0:64]), "dv")
        S.mark("ldtT_%d" % q, "dv")
        S.wmark(sc, "cumT_%d" % q)
        S.inc(sc.activation(vT_all[:, q * 64:(q + 1) * 64], cumT[:, :], AF.Exp), "ac")
        S.mark("vT_%d" % q, "ac")
        S.wait(ve, "dv")
        S.inc(ve.tensor_reduce(stat[0:64, 7:8], logdA_sb[0:64, qs], AX.X, ALU.add), "dv")
        S.mark("cq_red_%d" % q, "dv")
        S.wmark(te, "cq_red_%d" % q)
        S.inc(te.transpose(P[2][0:1, 0:64], stat[0:64, 7:8], ident64[:, :]), "pe")
        S.mark("cqrow_ps", "pe")
        S.wmark(ve, "cqrow_ps")
        S.inc(ve.tensor_copy(crow[:, 0:64], P[2][0:1, 0:64]), "dv")
        S.mark("cqrow_%d" % q, "dv")
        S.wmark(te, "cqrow_%d" % q)
        S.inc(te.matmul(P[2][:, 0:64], ones1f[:, :], crow[:, 0:64], start=True, stop=True), "pe")
        S.mark("cqb_ps", "pe")
        S.wmark(ve, "cqb_ps")
        S.wmark(ve, "ldtT_%d" % q)
        S.inc(ve.tensor_tensor(gT[:, :], P[2][:, 0:64], cumT[:, :], ALU.subtract), "dv")
        S.mark("cqbread_%d" % q, "dv")
        S.wmark(sc, "cqbread_%d" % q)
        S.inc(sc.activation(ecq_all[:, q * 64:(q + 1) * 64], P[2][:, 0:64], AF.Exp), "ac")
        S.mark("ecq_%d" % q, "ac")
        S.wait(ve, "dv")
        S.inc(ve.tensor_tensor(gT[:, :], gT[:, :], ldtT[:, :], ALU.add), "dv")
        S.mark("gt_pre", "dv")
        S.wmark(sc, "gt_pre")
        S.wmark(sc, "ecq_%d" % q)
        S.inc(sc.activation(gT[:, :], gT[:, :], AF.Exp), "ac")
        S.mark("gT_%d" % q, "ac")

        S.wmark_if(te, "bnat_cons_%d" % (q - 1))
        if q == 0:
            S.wmark(te, "ph1_done")
            S.wait(te, "ac")
        S.inc(te.transpose(p_ta[:, 0:128], bmT_sb[:, qs], ident_b[:, :]), "pe")
        S.mark("bnat_ps", "pe")
        S.wmark(ve, "bnat_ps")
        S.inc(ve.tensor_copy(bnat[:, :], p_ta[:, 0:128]), "dv")
        S.mark("bnat_%d" % q, "dv")
        S.wmark_if(te, "gts_cons_%d" % (q - 1))
        S.inc(te.matmul(P[3][:, 0:128], bmT_sb[:, qs], cmT_sb[:, qs], start=True, stop=True), "pe")
        S.mark("gts_ps", "pe")
        S.wmark(ve, "gts_ps")
        S.inc(ve.tensor_copy(gts[:, :], P[3][:, 0:128]), "dv")
        S.mark("gts_%d" % q, "dv")

        S.wmark(te, "xsld_%d" % q)
        S.wmark(te, "bnat_%d" % q)
        S.wmark_if(te, "xsnat_cons_%d" % (q - 1))
        for k in range(NKD):
            pt = p_ta if k % 2 == 0 else p_tb
            S.wmark_if(te, "xsnat_ev_%d" % (k - 2 if k >= 2 else k + 30))
            S.inc(te.transpose(pt[:, 0:128], xq[:, k * Q:(k + 1) * Q], ident_b[:, :]), "pe")
            S.mark("xsnat_ps_%d" % k, "pe")
            S.wmark(ve, "xsnat_ps_%d" % k)
            S.inc(ve.tensor_copy(xs_nat[:, k * 128:(k + 1) * 128], pt[:, 0:128]), "dv")
            S.mark("xsnat_ev_%d" % k, "dv")
        S.mark("xsnat_%d" % q, "dv")
        S.mark("xstq_free_%d" % (q % 2), "dv")

        for grp in range(8):
            py = P[4 + grp % 2]
            S.wmark(te, "hbf_%d" % (q - 1))
            S.wmark_if(te, "yiev_%d" % (grp % 2))
            S.inc(te.matmul(py[:, 0:512], cmT_sb[:, qs], hbf[:, grp * 512:(grp + 1) * 512], start=True, stop=True), "pe")
            S.mark("yimm_%d" % (grp % 2), "pe")
            S.wmark(ve, "yimm_%d" % (grp % 2))
            S.wmark(ve, "vT_%d" % q)
            for rr in range(8):
                r = grp * 8 + rr
                ysl = y_acc[:, q * DM + r * 64:q * DM + (r + 1) * 64]
                S.wait(ve, "dv")
                S.inc(ve.tensor_scalar(ysl, py[:, rr * 64:(rr + 1) * 64], vT_all[:, q * 64 + r:q * 64 + r + 1], None, ALU.mult), "dv")
            S.mark("yiev_%d" % (grp % 2), "dv")
        S.mark("yinter_%d" % q, "dv")
        S.mark("p45_free", "dv")

        S.wmark(te, "xsnat_%d" % q)
        for r in range(NH):
            pcb = P[r % 2]
            py2 = P[2 + r % 2]
            mt = mt_sb[r % 2]
            xw = xsw[r % 2]
            S.wmark_if(te, "me_%d" % (r - 2))
            if r == 0:
                S.mark("p0_free", "pe")
                S.wmark(te, "cumT_%d" % q)
            if r <= 1:
                S.wmark_if(te, "crow_cons_prev")
            S.wmark_if(te, "crow_%d" % (r - 1))
            S.inc(te.transpose(pcb[0:1, 0:128], cumT[:, r:r + 1], ident_f[:, :]), "pe")
            S.mark("crow_ps", "pe")
            S.wmark(ve, "crow_ps")
            S.wmark_if(ve, "crow_cons_%d" % (r - 1))
            S.inc(ve.tensor_copy(crow[:, :], pcb[0:1, 0:128]), "dv")
            S.mark("crow_%d" % r, "dv")
            S.wmark(te, "crow_%d" % r)
            S.inc(te.matmul(pcb[:, 0:128], ones1f[:, :], crow[:, :], start=True, stop=True), "pe")
            S.mark("cb_%d" % r, "pe")
            S.mark("crow_cons_%d" % r, "pe")
            S.wmark(ve, "cb_%d" % r)
            S.wmark_if(ve, "mt_cons_%d" % (r - 2))
            S.wait(ve, "dv")
            S.inc(ve.tensor_scalar(me_sb[:, :], pcb[:, 0:128], cumT[:, r:r + 1], ldtT[:, r:r + 1], ALU.subtract, ALU.add), "dv")
            S.mark("me_%d" % r, "dv")
            S.wait(ve, "dv")
            S.inc(ve.tensor_tensor(me_sb[:, :], me_sb[:, :], mask_sb[:, :], ALU.add), "dv")
            S.wait(sc, "dv")
            S.inc(sc.activation(me_sb[:, :], me_sb[:, :], AF.Exp), "ac")
            S.wait(ve, "ac")
            S.wmark(ve, "gts_%d" % q)
            S.inc(ve.tensor_tensor(mt[:, :], me_sb[:, :], gts[:, :], ALU.mult), "dv")
            S.mark("mt_%d" % r, "dv")
            S.wmark(te, "mt_%d" % r)
            S.wmark_if(te, "y2ev_%d" % (r - 2))
            S.wmark_if(te, "hup_%d" % (r - 2))
            S.inc(te.matmul(py2[:, 0:64], mt[:, :], xs_nat[:, r * 64:(r + 1) * 64], start=True, stop=True), "pe")
            S.mark("y2_%d" % r, "pe")
            S.mark("mt_cons_%d" % r, "pe")
            ysl = y_acc[:, q * DM + r * 64:q * DM + (r + 1) * 64]
            S.wmark(ve, "y2_%d" % r)
            S.wmark(ve, "yinter_%d" % q)
            S.wait(ve, "dv")
            S.inc(ve.tensor_tensor(ysl, ysl, py2[:, 0:64], ALU.add), "dv")
            S.wait(ve, "dv")
            S.inc(ve.scalar_tensor_tensor(ysl, xs_nat[:, r * 64:(r + 1) * 64], dbc_sb[:, r:r + 1], ysl, ALU.mult, ALU.add), "dv")
            S.mark("y2ev_%d" % r, "dv")
            S.wmark(ve, "gT_%d" % q)
            S.wmark_if(ve, "xsw_cons_%d" % (r - 2))
            S.wait(ve, "dv")
            S.inc(ve.tensor_scalar(xw[:, :], xs_nat[:, r * 64:(r + 1) * 64], gT[:, r:r + 1], None, ALU.mult), "dv")
            S.mark("xsw_%d" % r, "dv")
            S.wmark(te, "xsw_%d" % r)
            S.wmark(te, "bnat_%d" % q)
            S.wmark(te, "y2ev_%d" % r)
            S.inc(te.matmul(py2[:, 64:128], bnat[:, :], xw[:, :], start=True, stop=True), "pe")
            S.mark("sprime_%d" % r, "pe")
            S.mark("xsw_cons_%d" % r, "pe")
            S.wmark(ve, "sprime_%d" % r)
            S.wmark(ve, "ecq_%d" % q)
            S.wait(ve, "dv")
            S.inc(ve.scalar_tensor_tensor(hfp[:, r * 64:(r + 1) * 64], hfp[:, r * 64:(r + 1) * 64], ecq_all[:, q * 64 + r:q * 64 + r + 1], py2[:, 64:128], ALU.mult, ALU.add), "dv")
            S.wait(ve, "dv")
            S.inc(ve.tensor_copy(hbf[:, r * 64:(r + 1) * 64], hfp[:, r * 64:(r + 1) * 64]), "dv")
            S.mark("hup_%d" % r, "dv")
        S.mark("hbf_%d" % q, "dv")
        S.mark("cum_cons_%d" % q, "pe")
        S.mark("cumT_cons_%d" % q, "pe")
        S.mark("bnat_cons_%d" % q, "pe")
        S.mark("gts_cons_%d" % q, "pe")
        S.mark("xsnat_cons_%d" % q, "pe")
        S.marks["p1_free"] = ("dv", S.c["dv"])

    cid = sy.partition_id()
    mine = cid % 2
    other = (cid + 1) % 2
    S.wmark(sy, "hbf_%d" % (NQ - 1))
    S.inc(sy.dma_start(out=shH[bass.ds(mine * 128, 128), :], in_=hbf[:, :]), "hsh", 16)
    S.wait(sy, "hsh")
    S.inc(sy.dma_start(out=shF[bass.ds(mine, 1), :], in_=nonce_sb[:, :]), "hsh", 16)
    S.wait(sy, "hsh")
    S.new("pol")
    with sy.register("cont") as cont, sy.register("expv") as expv, sy.register("fv") as fv, sy.register("nv") as nv:
        sy.reg_load(nv, nonce_sb[0:1, 0:1])
        sy.reg_mov(cont, 1)
        sy.reg_mov(expv, 0)
        with sy.While(cont):
            sy.reg_add(expv, expv, 16)
            sy.dma_start(out=pollbuf[:, :], in_=shF[bass.ds(other, 1), :]).then_inc(S.h["pol"], 16)
            sy.wait_ge(S.h["pol"], expv)
            sy.reg_load(fv, pollbuf[0:1, 0:1])
            sy.reg_alu(cont, fv, nv, ALU.not_equal)
    S.inc(sy.dma_start(out=hg[:, :], in_=shH[bass.ds(other * 128, 128), :]), "hsh", 16)
    S.wait(ve, "hsh")
    S.wait(ve, "dv")
    S.inc(ve.tensor_scalar(hg[:, :], hg[:, :], flag_sb[:, 0:1], None, ALU.mult), "dv")
    S.wait(ve, "dv")
    S.inc(ve.memset(gamb[:, :], 1.0), "dv")
    S.mark("hg_ready", "dv")

    for q in range(NQ):
        qs = slice(q * Q, (q + 1) * Q)
        S.wait(ve, "dv")
        S.inc(ve.tensor_tensor(sTq[:, :], vT_all[:, q * 64:(q + 1) * 64], gamb[:, :], ALU.mult), "dv")
        S.mark("sTq_%d" % q, "dv")
        for grp in range(8):
            py = P[4 + grp % 2]
            S.wmark(te, "hg_ready")
            S.wmark_if(te, "dyev_%d" % (grp % 2))
            if q == 0 and grp <= 1:
                S.wmark(te, "p45_free")
            S.inc(te.matmul(py[:, 0:512], cmT_sb[:, qs], hg[:, grp * 512:(grp + 1) * 512], start=True, stop=True), "pe")
            S.mark("dymm_%d" % (grp % 2), "pe")
            S.wmark(ve, "dymm_%d" % (grp % 2))
            S.wmark(ve, "sTq_%d" % q)
            for rr in range(8):
                r = grp * 8 + rr
                ysl = y_acc[:, q * DM + r * 64:q * DM + (r + 1) * 64]
                S.wait(ve, "dv")
                S.inc(ve.scalar_tensor_tensor(ysl, py[:, rr * 64:(rr + 1) * 64], sTq[:, r:r + 1], ysl, ALU.mult, ALU.add), "dv")
            S.mark("dyev_%d" % (grp % 2), "dv")
        S.wait(ve, "dv")
        S.inc(ve.tensor_tensor(gamb[:, :], gamb[:, :], ecq_all[:, q * 64:(q + 1) * 64], ALU.mult), "dv")
    S.mark("y_done", "dv")

    if "y" in dbg:
        d = dbg_t("y", [128, NQ * DM], BF16)
        S.wmark(sy, "y_done")
        S.inc(sy.dma_start(out=d, in_=y_acc[:, :]), "sp", 16)
    if "H" in dbg:
        d = dbg_t("H", [128, NH * HD], BF16)
        S.wmark(sy, "hbf_%d" % (NQ - 1))
        S.inc(sy.dma_start(out=d, in_=hbf[:, :]), "sp", 16)

    sfree("crow", "xsw1", "xsw0", "mt_sb1", "mt_sb0", "me_sb", "gts", "sTq", "gamb",
          "ecq_all", "vT_all", "gT", "ldtT", "cumT", "cumc", "bnat", "xs_nat",
          "xstq1", "xstq0", "hg", "hbf", "hfp")

    if stop_after == "ph2":
        S.wait(sy, "sp")
        S.barrier(ALL_E)
        return nc, dbg_aps
    S.barrier(ALL_E)
    ynT = salloc("ynT", [128, NKD * T], BF16, side="right")
    ztq = [salloc("ztq%d" % i, [128, 32 * Q], BF16) for i in range(2)]
    sz = salloc("sz", [128, DM], BF16)
    yg = salloc("yg", [128, DM], F32)
    ynb = salloc("ynb", [128, DM], BF16)

    for q in range(NQ):
        zq = ztq[q % 2]
        S.wmark_if(sy, "ztq_free_%d" % (q % 2))
        for a in range(4):
            S.inc(sy.dma_start(out=zq[:, a * 8 * Q:(a + 1) * 8 * Q].rearrange("p (m t) -> p m t", m=8),
                               in_=zT_sp[a * 8:(a + 1) * 8, :, q * Q:(q + 1) * Q].rearrange("m p t -> p m t")), "sp2x%d" % (q % 2), 16)
        S.mark("zld_%d" % q, "sp2x%d" % (q % 2))
        S.wmark(te, "zld_%d" % q)
        S.wmark_if(te, "sz_cons_%d" % (q - 1))
        for k in range(NKD):
            pt = p_ta if k % 2 == 0 else p_tb
            S.wmark_if(te, "zsilu_%d" % (k - 2 if k >= 2 else k + 30))
            if k < 2:
                S.wmark_if(te, "ynT_ev_%d" % (k + 30))
            S.inc(te.transpose(pt[:, 0:128], zq[:, k * Q:(k + 1) * Q], ident_b[:, :]), "pe")
            S.mark("zT_ps_%d" % k, "pe")
            S.wmark(sc, "zT_ps_%d" % k)
            S.wait(sc, "ac")
            S.inc(sc.activation(sz[:, k * 128:(k + 1) * 128], pt[:, 0:128], AF.Silu), "ac")
            S.mark("zsilu_%d" % k, "ac")
        S.mark("ztq_free_%d" % (q % 2), "ac")
        S.wait(ve, "ac")
        S.wait(ve, "dv")
        S.inc(ve.tensor_tensor(yg[:, :], y_acc[:, q * DM:(q + 1) * DM], sz[:, :], ALU.mult), "dv")
        S.mark("sz_cons_%d" % q, "dv")
        S.wait(sc, "dv")
        S.inc(sc.activation(sz[:, :], yg[:, :], AF.Square, accum_out=stat[:, 0:1]), "ac")
        S.wait(ve, "ac")
        S.inc(ve.tensor_scalar(stat[:, 0:1], stat[:, 0:1], 1.0 / DM, EPS, ALU.mult, ALU.add), "dv")
        S.wait(sc, "dv")
        S.inc(sc.activation(stat[:, 1:2], stat[:, 0:1], AF.Sqrt), "ac")
        S.wait(ve, "ac")
        S.inc(ve.reciprocal(stat[:, 2:3], stat[:, 1:2]), "dv")
        S.wait(ve, "dv")
        S.inc(ve.tensor_scalar(ynb[:, :], yg[:, :], stat[:, 2:3], None, ALU.mult), "dv")
        S.mark("ynb_%d" % q, "dv")
        S.wmark(te, "ynb_%d" % q)
        for k in range(NKD):
            pt = p_ta if k % 2 == 0 else p_tb
            S.wmark_if(te, "ynT_ev_%d" % (k - 2 if k >= 2 else k + 30))
            S.inc(te.transpose(pt[:, 0:128], ynb[:, k * 128:(k + 1) * 128], ident_b[:, :]), "pe")
            S.mark("ynT_ps_%d" % k, "pe")
            S.wmark(ve, "ynT_ps_%d" % k)
            S.inc(ve.tensor_scalar(ynT[:, k * T + q * Q:k * T + (q + 1) * Q], pt[:, 0:128], nwcol_sb[:, k:k + 1], None, ALU.mult), "dv")
            S.mark("ynT_ev_%d" % k, "dv")
    S.mark("ynT_done", "dv")

    if "ynT" in dbg:
        d = dbg_t("ynT", [128, NKD * T], BF16)
        S.wmark(sy, "ynT_done")
        S.inc(sy.dma_start(out=d, in_=ynT[:, :]), "sp", 16)

    sfree("ynb", "yg", "sz", "ztq1", "ztq0", "y_acc")
    wo = [salloc("wo%d" % i, [128, NKD * 512], BF16) for i in range(2)]
    sstg = salloc("sstg", [128, 512], BF16)
    S.mark("preop_pe", "pe")
    S.mark("preop_ac", "ac")
    for dc in range(8):
        wob = wo[dc % 2]
        if dc == 0:
            S.wmark(gp, "ynT_done")
            S.wmark(gp, "preop_pe")
            S.wmark(gp, "preop_ac")
        S.wmark_if(gp, "wofree_%d" % (dc % 2))
        for kk in range(4):
            S.inc(gp.dma_start(out=wob[:, kk * 8 * 512:(kk + 1) * 8 * 512].rearrange("p (k c) -> p k c", k=8),
                               in_=w_out[kk * 8 * 128:(kk + 1) * 8 * 128, dc * 512:(dc + 1) * 512].rearrange("(k p) c -> p k c", p=128)), "gldx%d" % (dc % 2), 16)
        S.mark("wold_%d" % dc, "gldx%d" % (dc % 2))
        S.wmark(te, "wold_%d" % dc)
        if dc == 0:
            S.wmark(te, "ynT_done")
        for tq in range(NQ):
            py = P[4 + tq % 2]
            S.wmark_if(te, "sev_%d" % (tq % 2))
            for k in range(NKD):
                ins = te.matmul(py[:, 0:512], ynT[:, k * T + tq * Q:k * T + (tq + 1) * Q], wob[:, k * 512:(k + 1) * 512],
                                start=(k == 0), stop=(k == NKD - 1))
            S.inc(ins, "pe")
            S.mark("smm_%d" % (tq % 2), "pe")
            S.wmark(ve, "smm_%d" % (tq % 2))
            S.wmark_if(ve, "sspill")
            S.wait(ve, "dv")
            S.inc(ve.tensor_copy(sstg[:, :], py[:, 0:512]), "dv")
            S.mark("sev_%d" % (tq % 2), "dv")
            S.wait(sy, "dv")
            S.inc(sy.dma_start(out=s_sp[tq, :, dc * 512:(dc + 1) * 512], in_=sstg[:, :]), "sp", 16)
            S.mark("sspill", "sp")
        S.marks["wofree_%d" % (dc % 2)] = ("pe", S.c["pe"])
    S.mark("s_done", "sp")

    if "s" in dbg:
        d = dbg_t("s", [NQ, 128, DM], BF16)
        S.wmark(sy, "s_done")
        S.inc(sy.dma_start(out=d, in_=s_sp), "sp", 16)

    if stop_after == "ph3":
        S.wait(sy, "sp")
        S.barrier(ALL_E)
        return nc, dbg_aps
    S.barrier(ALL_E)
    sfree("sstg", "wo1", "wo0", "ynT", "cmT_sb", "bmT_sb", "logdt_sb", "logdA_sb", "dts_sb")
    xnT = salloc("xnT", [128, NKD * T], BF16)
    hact = salloc("hact", [128, 29 * T], BF16)
    s2all = salloc("s2all", [128, 32 * 172], F32)
    s1b = salloc("s1b", [128, GRP], F32)
    s3b = salloc("s3b", [128, GRP], F32)
    pw_i = [salloc("pw_i%d" % i, [128, 8 * GRP], I32) for i in range(2)]
    pw_s1 = salloc("pw_s1", [128, 8 * GRP], I32)
    dq_big = [salloc("dq_big%d" % i, [128, 1024], BF16) for i in range(2)]
    w1t = [salloc("w1t%d" % i, [128, 1024], BF16) for i in range(4)]
    w2t_dc = salloc("w2t_dc", [128, 29 * 512], BF16)
    silu_t = salloc("silu_t", [128, 512], F32)
    gin = salloc("gin", [128, 512], F32)
    sstg2 = salloc("sstg2", [128, 512], BF16)
    xstg2 = salloc("xstg2", [128, 512], F32)
    ostg = salloc("ostg", [128, 512], F32)

    for a in range(8):
        S.inc(sy.dma_start(out=xnT[:, a * 4 * T:(a + 1) * 4 * T].rearrange("p (m t) -> p m t", m=4),
                           in_=xnT_sp[a * 4:(a + 1) * 4, :, :].rearrange("m p t -> p m t")), "xnld", 16)
    S.mark("xnT_ld", "xnld")
    S.inc(sy.dma_start(out=s2all[:, :].rearrange("p (db c) -> p db c", c=172), in_=s2q.rearrange("(db p c) -> p db c", p=128, c=172)), "xnld", 16)
    S.mark("s2_ld", "xnld")

    w13 = [(w1q, s1b), (w3q, s3b)]
    sqs_ = {0: s1q, 1: s3q}
    for pi, hts in enumerate(PASS_HT):
        hts = list(hts)
        nht = len(hts)
        for hi, ht in enumerate(hts):
            S.wmark_if(sy, "sb_cons")
            S.inc(sy.dma_start(out=s1b[:, :], in_=s1q.rearrange("(h c) -> h c", c=GRP)[128 * ht:128 * (ht + 1), :]), "sbld", 16)
            S.inc(sy.dma_start(out=s3b[:, :], in_=s3q.rearrange("(h c) -> h c", c=GRP)[128 * ht:128 * (ht + 1), :]), "sbld", 16)
            S.mark("sb_ld", "sbld")
            for wi, (wq, sb_) in enumerate(w13):
                for gq in range(4):
                    slot = gq % 2
                    pw = pw_i[slot]
                    pws = pw_s1
                    dqb = dq_big[slot]
                    wt = w1t[wi * 2 + slot]
                    S.wmark_if(sy, "pw_free_%d" % slot)
                    S.inc(sy.dma_start(out=pw[:, :],
                                       in_=wq.rearrange("(h k) c -> h k c", k=32)[128 * ht:128 * (ht + 1), 8 * gq:8 * gq + 8, :]), "pwx%d" % slot, 16)
                    S.mark("pwld", "pwx%d" % slot)
                    S.wmark(ve, "pwld")
                    S.wmark(ve, "sb_ld")
                    S.wmark_if(ve, "dq_free_%d" % slot)
                    dqv = dqb[:].rearrange("p (k h) -> p k h", h=128)
                    pwv = pws[:].rearrange("p (k c) -> p k c", c=GRP)
                    smv = sb_[:, 16 * gq:16 * gq + 16:2].rearrange("p (k o) -> p k o", o=1).broadcast_to([128, 8, GRP])
                    slv = sb_[:, 16 * gq + 1:16 * gq + 16:2].rearrange("p (k o) -> p k o", o=1).broadcast_to([128, 8, GRP])
                    S.wait(ve, "dv")
                    S.inc(ve.tensor_scalar(pws[:, :], pw[:, :], 4, None, ALU.arith_shift_right), "dv")
                    S.wait(ve, "dv")
                    S.inc(ve.tensor_tensor(dqv[:, :, 0:GRP], pwv, smv, ALU.mult), "dv")
                    S.wait(ve, "dv")
                    S.inc(ve.tensor_scalar(pws[:, :], pw[:, :], 15, 8, ALU.bitwise_and, ALU.bitwise_xor), "dv")
                    S.mark("pw_free_%d" % slot, "dv")
                    S.wait(ve, "dv")
                    S.inc(ve.scalar_tensor_tensor(dqv[:, :, GRP:128], pwv, 8.0, slv, ALU.subtract, ALU.mult), "dv")
                    S.mark("dq_%d" % slot, "dv")
                    S.wmark(sy, "dq_%d" % slot)
                    S.wmark_if(sy, "wt_cons_%d" % (wi * 2 + slot))
                    for kk in range(8):
                        S.inc(sy.dma_start_transpose(wt[:, kk * 128:(kk + 1) * 128], dqb[:, kk * 128:(kk + 1) * 128]), "tdq", 16)
                    S.mark("dq_free_%d" % slot, "tdq")
                    S.mark("wtT_%d" % (wi * 2 + slot), "tdq")
                    pa = P[wi * 2]
                    pb = P[wi * 2 + 1]
                    S.wmark(te, "wtT_%d" % (wi * 2 + slot))
                    if gq == 0:
                        S.wmark(te, "xnT_ld" if (pi == 0 and hi == 0) else "f1ev_%d" % wi)
                    for tc in range(2):
                        pp = pa if tc == 0 else pb
                        for kk in range(8):
                            kd = 8 * gq + kk
                            ins = te.matmul(pp[:, 0:512],
                                            wt[:, kk * 128:(kk + 1) * 128],
                                            xnT[:, kd * T + tc * 512:kd * T + (tc + 1) * 512],
                                            start=(kd == 0), stop=(kd == NKD - 1))
                    S.inc(ins, "pe")
                    S.mark("wt_cons_%d" % (wi * 2 + slot), "pe")
                S.mark("f1mm_%d" % wi, "pe")
            S.mark("sb_cons", "dv")
            for tc in range(2):
                S.wmark(sc, "f1mm_0")
                S.wmark(sc, "f1mm_1")
                S.wmark_if(sc, "siluT_free")
                S.wait(sc, "ac")
                S.inc(sc.activation(silu_t[:, :], P[0 if tc == 0 else 1][:, 0:512], AF.Silu), "ac")
                S.wait(ve, "ac")
                S.wait(ve, "dv")
                S.inc(ve.tensor_tensor(hact[:, hi * T + tc * 512:hi * T + (tc + 1) * 512], silu_t[:, :],
                                       P[2 if tc == 0 else 3][:, 0:512], ALU.mult), "dv")
                S.mark("siluT_free", "dv")
            S.mark("f1ev_0", "dv")
            S.mark("f1ev_1", "dv")
        S.mark("ffn1_p%d" % pi, "dv")
        if pi == 0 and "hact" in dbg:
            d = dbg_t("hact", [128, 29 * T], BF16)
            S.wmark(sy, "ffn1_p0")
            S.inc(sy.dma_start(out=d, in_=hact[:, :]), "sp", 16)
            S.wait(sy, "sp")

        for dc in range(8):
            for dbl in range(4):
                db = 4 * dc + dbl
                slot = dbl % 2
                pw = pw_i[slot]
                pws = pw_s1
                dqb = dq_big[slot]
                ng2 = (nht + 7) // 8
                for g2q in range(ng2):
                    k20 = hts[0] + 8 * g2q
                    nk2 = min(8, hts[0] + nht - k20)
                    S.wmark_if(sy, "pw_free_%d" % slot)
                    S.inc(sy.dma_start(out=pw[:, 0:nk2 * GRP],
                                       in_=w2q.rearrange("(d k) c -> d k c", k=HID // 128)[128 * db:128 * (db + 1), k20:k20 + nk2, :]), "pwx%d" % slot, 16)
                    S.mark("pwld2", "pwx%d" % slot)
                    S.wmark(ve, "pwld2")
                    if pi == 0 and dc == 0 and dbl == 0 and g2q == 0:
                        S.wmark(ve, "s2_ld")
                    S.wmark_if(ve, "dq_free_%d" % slot)
                    dqv = dqb[:, 0:nk2 * 128].rearrange("p (k h) -> p k h", h=128)
                    pwv = pws[:, 0:nk2 * GRP].rearrange("p (k c) -> p k c", c=GRP)
                    smv = s2all[:, db * 172 + 2 * k20:db * 172 + 2 * (k20 + nk2):2].rearrange("p (k o) -> p k o", o=1).broadcast_to([128, nk2, GRP])
                    slv = s2all[:, db * 172 + 2 * k20 + 1:db * 172 + 2 * (k20 + nk2):2].rearrange("p (k o) -> p k o", o=1).broadcast_to([128, nk2, GRP])
                    S.wait(ve, "dv")
                    S.inc(ve.tensor_scalar(pws[:, 0:nk2 * GRP], pw[:, 0:nk2 * GRP], 4, None, ALU.arith_shift_right), "dv")
                    S.wait(ve, "dv")
                    S.inc(ve.tensor_tensor(dqv[:, :, 0:GRP], pwv, smv, ALU.mult), "dv")
                    S.wait(ve, "dv")
                    S.inc(ve.tensor_scalar(pws[:, 0:nk2 * GRP], pw[:, 0:nk2 * GRP], 15, 8, ALU.bitwise_and, ALU.bitwise_xor), "dv")
                    S.mark("pw_free_%d" % slot, "dv")
                    S.wait(ve, "dv")
                    S.inc(ve.scalar_tensor_tensor(dqv[:, :, GRP:128], pwv, 8.0, slv, ALU.subtract, ALU.mult), "dv")
                    S.mark("dq2_%d" % slot, "dv")
                    S.wmark(sy, "dq2_%d" % slot)
                    S.wmark_if(sy, "w2tmp_cons")
                    for kk in range(nk2):
                        S.inc(sy.dma_start_transpose(dq_big[1 - slot][:, kk * 128:(kk + 1) * 128], dqb[:, kk * 128:(kk + 1) * 128]), "tdq", 16)
                    S.mark("w2T_ps", "tdq")
                    S.wmark(ve, "w2T_ps")
                    if pi > 0 or dc > 0:
                        S.wmark_if(ve, "gmm_done")
                    for kk in range(nk2):
                        k2l = k20 - hts[0] + kk
                        S.wait(ve, "dv")
                        S.inc(ve.tensor_copy(w2t_dc[:, k2l * 512 + dbl * 128:k2l * 512 + (dbl + 1) * 128],
                                             dq_big[1 - slot][:, kk * 128:(kk + 1) * 128]), "dv")
                    S.mark("w2tmp_cons", "dv")
                    S.mark("dq_free_%d" % slot, "dv")
                    S.mark("dq_free_%d" % (1 - slot), "dv")
            S.mark("w2dc_%d" % dc, "dv")
            if pi == 0 and dc == 0 and "w2t" in dbg:
                d = dbg_t("w2t", [128, 29 * 512], BF16)
                S.wmark(sy, "w2dc_0")
                S.inc(sy.dma_start(out=d, in_=w2t_dc[:, :]), "sp", 16)
                S.wait(sy, "sp")
            for ti in range(NQ):
                pg = P[4 + ti % 2]
                S.wmark(te, "w2dc_%d" % dc)
                S.wmark(te, "ffn1_p%d" % pi)
                S.wmark_if(te, "gev_%d" % (ti % 2))
                if pi == 0 and dc == 0 and ti <= 1:
                    S.wmark_if(te, "sev_0")
                    S.wmark_if(te, "sev_1")
                for hi in range(nht):
                    ins = te.matmul(pg[:, 0:512], hact[:, hi * T + ti * 128:hi * T + (ti + 1) * 128],
                                    w2t_dc[:, hi * 512:(hi + 1) * 512],
                                    start=(hi == 0), stop=(hi == nht - 1))
                S.inc(ins, "pe")
                S.mark("gmm_%d" % (ti % 2), "pe")
                if pi == 0:
                    S.wmark(ve, "gmm_%d" % (ti % 2))
                    S.wmark_if(ve, "gspill")
                    S.wait(ve, "dv")
                    S.inc(ve.tensor_copy(gin[:, :], pg[:, 0:512]), "dv")
                    S.mark("gev_%d" % (ti % 2), "dv")
                    S.wait(sy, "dv")
                    S.inc(sy.dma_start(out=g1_sp[ti, :, dc * 512:(dc + 1) * 512], in_=gin[:, :]), "sp", 16)
                    S.mark("gspill", "sp")
                    S.mark("gin_free", "sp")
                elif pi == 1:
                    S.wmark_if(sy, "gin_free")
                    if dc == 0 and ti == 0:
                        S.wait(sy, "sp")
                    S.inc(sy.dma_start(out=gin[:, :], in_=g1_sp[ti, :, dc * 512:(dc + 1) * 512]), "ginld", 16)
                    S.wmark(ve, "gmm_%d" % (ti % 2))
                    S.wait(ve, "ginld")
                    S.wmark_if(ve, "gspill")
                    S.wait(ve, "dv")
                    S.inc(ve.tensor_tensor(gin[:, :], gin[:, :], pg[:, 0:512], ALU.add), "dv")
                    S.mark("gev_%d" % (ti % 2), "dv")
                    S.mark("gin_free", "dv")
                    S.wait(sy, "dv")
                    S.inc(sy.dma_start(out=g2_sp[ti, :, dc * 512:(dc + 1) * 512], in_=gin[:, :]), "sp", 16)
                    S.mark("gspill", "sp")
                    S.mark("gin_free", "sp")
                else:
                    S.wmark_if(sy, "gin_free")
                    S.wmark_if(sy, "fin_done")
                    if dc == 0 and ti == 0:
                        S.wait(sy, "sp")
                        S.wmark(sy, "s_done")
                    S.inc(sy.dma_start(out=gin[:, :], in_=g2_sp[ti, :, dc * 512:(dc + 1) * 512]), "ginld", 16)
                    S.inc(sy.dma_start(out=sstg2[:, :], in_=s_sp[ti, :, dc * 512:(dc + 1) * 512]), "ginld", 16)
                    S.inc(sy.dma_start(out=xstg2[:, :], in_=x_pad[ET + ti * 128:ET + (ti + 1) * 128, dc * 512:(dc + 1) * 512]), "ginld", 16)
                    S.wmark(ve, "gmm_%d" % (ti % 2))
                    S.wait(ve, "ginld")
                    S.wait(ve, "dv")
                    S.inc(ve.tensor_tensor(gin[:, :], gin[:, :], pg[:, 0:512], ALU.add), "dv")
                    S.mark("gev_%d" % (ti % 2), "dv")
                    S.mark("gin_free", "dv")
                    S.wait(sc, "dv")
                    S.inc(sc.activation(gin[:, :], gin[:, :], AF.Silu), "ac")
                    S.wait(ve, "ac")
                    S.wait(ve, "dv")
                    S.inc(ve.tensor_tensor(ostg[:, :], gin[:, :], sstg2[:, :], ALU.mult), "dv")
                    S.wait(ve, "dv")
                    S.inc(ve.tensor_tensor(ostg[:, :], ostg[:, :], xstg2[:, :], ALU.add), "dv")
                    S.wait(sy, "dv")
                    S.inc(sy.dma_start(out=og[ti * 128:(ti + 1) * 128, dc * 512:(dc + 1) * 512], in_=ostg[:, :]), "out", 16)
                    S.mark("fin_done", "out")
            S.mark("gmm_done", "pe")

    if "g1" in dbg:
        d = dbg_t("g1", [NQ, 128, DM])
        S.wait(sy, "sp")
        S.inc(sy.dma_start(out=d, in_=g1_sp), "sp", 16)
    if "gfin" in dbg:
        d = dbg_t("gfin", [NQ, 128, DM])
        S.wait(sy, "sp")
        S.inc(sy.dma_start(out=d, in_=g2_sp), "sp", 16)

    S.wait(sy, "out")
    return nc, dbg_aps


def _shard_inputs(inputs):
    x = np.ascontiguousarray(inputs["L0_attn_hidden_states"])
    rng = np.random.default_rng()
    nv = int(rng.integers(1, 2**31 - 1))
    in_maps = []
    for c in range(8):
        b, half = c // 2, c % 2
        xp = np.zeros((TT, DM), dtype=np.float32)
        if half == 0:
            xp[ET:] = x[b, 0:T]
        else:
            xp[0:ET] = x[b, T - ET:T]
            xp[ET:] = x[b, T:2 * T]
        in_maps.append({
            "x_pad": xp,
            "w_in": np.ascontiguousarray(inputs["L0_attn_in_proj_w"], dtype=np.float32),
            "conv_w": np.ascontiguousarray(inputs["L0_attn_conv_w"], dtype=np.float32),
            "a_log": np.ascontiguousarray(inputs["L0_attn_A_log"], dtype=np.float32),
            "dt_bias": np.ascontiguousarray(inputs["L0_attn_dt_bias"], dtype=np.float32),
            "d_vec": np.ascontiguousarray(inputs["L0_attn_D"], dtype=np.float32),
            "norm_w": np.ascontiguousarray(inputs["L0_attn_norm_w"], dtype=np.float32),
            "ln_w": np.ascontiguousarray(inputs["L0_attn_ln_w"], dtype=np.float32),
            "mup": np.ascontiguousarray(inputs["L0_attn_mup_vector"], dtype=np.float32),
            "w_out": np.ascontiguousarray(inputs["L0_attn_out_proj_w"], dtype=np.float32),
            "w1q": np.ascontiguousarray(inputs["L0_ffn_w1"], dtype=np.int32),
            "s1q": np.ascontiguousarray(inputs["L0_ffn_s1"], dtype=np.float32),
            "w2q": np.ascontiguousarray(inputs["L0_ffn_w2"], dtype=np.int32),
            "s2q": np.ascontiguousarray(inputs["L0_ffn_s2"], dtype=np.float32),
            "w3q": np.ascontiguousarray(inputs["L0_ffn_w3"], dtype=np.int32),
            "s3q": np.ascontiguousarray(inputs["L0_ffn_s3"], dtype=np.float32),
            "flagv": np.full((128, 1), float(half), dtype=np.float32),
            "nonce": np.full((1, 16), nv, dtype=np.int32),
        })
    return in_maps


def kernel(**inputs):
    nc, _ = build_nc()
    in_maps = _shard_inputs(inputs)
    res = run_bass_kernel_spmd(nc, in_maps, core_ids=list(range(8)))
    out = np.empty((4, 2 * T, DM), dtype=np.float32)
    for c in range(8):
        b, half = c // 2, c % 2
        out[b, half * T:(half + 1) * T, :] = res.results[c]["og"]
    return out


# version 04 (final)
# speedup vs baseline: 1.9633x; 1.9633x over previous
import numpy as np
import ml_dtypes
import concourse.bass as bass
import concourse.mybir as mybir
from concourse.bass_utils import run_bass_kernel_spmd

F32 = mybir.dt.float32
BF16 = mybir.dt.bfloat16
I32 = mybir.dt.int32
AF = mybir.ActivationFunctionType
ALU = mybir.AluOpType
AX = mybir.AxisListType

DM = 4096
PROJ = 8512
CONV = 4352
NH = 64
HD = 64
KC = 4
HID = 11008
T = 1024
ET = 128
TT = T + ET
NQ = 8
Q = 128
NKD = 32
NMZ = 32
NMT = 67
GRP = 64
EPS = 1e-5
NMX = 34
PASS_HT = [range(0, 29), range(29, 58), range(58, 86)]
bfdt = ml_dtypes.bfloat16
ENGS = ("sld", "gld", "pe", "dv", "ac", "po", "sp", "sp2", "tdq", "hsh", "out")


class Sems:
    def __init__(self, nc):
        self.nc = nc
        self.h = {}
        self.c = {}
        self.marks = {}

    def new(self, *names):
        for name in names:
            self.h[name] = self.nc.semaphore(name).__enter__()
            self.c[name] = 0

    def inc(self, ins, name, n=1):
        ins.then_inc(self.h[name], n)
        self.c[name] += n
        return self.c[name]

    def wait(self, eng, name, val=None):
        eng.wait_ge(self.h[name], self.c[name] if val is None else val)

    def mark(self, label, name):
        self.marks[label] = (name, self.c[name])

    def wmark(self, eng, label):
        name, v = self.marks[label]
        eng.wait_ge(self.h[name], v)

    def wmark_if(self, eng, label):
        if label in self.marks:
            self.wmark(eng, label)

    def barrier(self, engs):
        snap = [(n, self.c[n]) for n in ENGS]
        for e in engs:
            for n, v in snap:
                if v > 0:
                    e.wait_ge(self.h[n], v)


def build_nc(debug=(), stop_after=None):
    dbg = set(debug)
    nc = bass.Bass("TRN2", target_bir_lowering=False)

    x_pad = nc.dram_tensor("x_pad", [TT, DM], F32, kind="ExternalInput").ap()
    w_in = nc.dram_tensor("w_in", [DM, PROJ], F32, kind="ExternalInput").ap()
    conv_w = nc.dram_tensor("conv_w", [CONV, KC], F32, kind="ExternalInput").ap()
    a_log = nc.dram_tensor("a_log", [NH], F32, kind="ExternalInput").ap()
    dt_bias = nc.dram_tensor("dt_bias", [NH], F32, kind="ExternalInput").ap()
    d_vec = nc.dram_tensor("d_vec", [NH], F32, kind="ExternalInput").ap()
    norm_w = nc.dram_tensor("norm_w", [DM], F32, kind="ExternalInput").ap()
    ln_w = nc.dram_tensor("ln_w", [DM], F32, kind="ExternalInput").ap()
    mup = nc.dram_tensor("mup", [PROJ], F32, kind="ExternalInput").ap()
    w_out = nc.dram_tensor("w_out", [DM, DM], F32, kind="ExternalInput").ap()
    w1q = nc.dram_tensor("w1q", [HID * DM // GRP // 2, GRP], I32, kind="ExternalInput").ap()
    s1q = nc.dram_tensor("s1q", [HID * DM // GRP], F32, kind="ExternalInput").ap()
    w2q = nc.dram_tensor("w2q", [HID * DM // GRP // 2, GRP], I32, kind="ExternalInput").ap()
    s2q = nc.dram_tensor("s2q", [HID * DM // GRP], F32, kind="ExternalInput").ap()
    w3q = nc.dram_tensor("w3q", [HID * DM // GRP // 2, GRP], I32, kind="ExternalInput").ap()
    s3q = nc.dram_tensor("s3q", [HID * DM // GRP], F32, kind="ExternalInput").ap()
    flagv = nc.dram_tensor("flagv", [128, 1], F32, kind="ExternalInput").ap()
    nonce = nc.dram_tensor("nonce", [1, 16], I32, kind="ExternalInput").ap()
    og = nc.dram_tensor("og", [T, DM], F32, kind="ExternalOutput").ap()

    zT_sp = nc.dram_tensor("zT_sp", [NMZ, 128, T], BF16, kind="Internal").ap()
    xsT_sp = nc.dram_tensor("xsT_sp", [NMZ, 128, T], BF16, kind="Internal").ap()
    xnT_sp = nc.dram_tensor("xnT_sp", [NKD, 128, T], BF16, kind="Internal").ap()
    s_sp = nc.dram_tensor("s_sp", [NQ, 128, DM], BF16, kind="Internal").ap()
    g1_sp = nc.dram_tensor("g1_sp", [NQ, 128, DM], F32, kind="Internal").ap()
    g2_sp = nc.dram_tensor("g2_sp", [NQ, 128, DM], F32, kind="Internal").ap()
    shH = nc.dram_tensor("shH", [256, NH * HD], BF16, kind="Internal", addr_space="Shared").ap()
    shF = nc.dram_tensor("shF", [2, 16], I32, kind="Internal", addr_space="Shared").ap()

    id128b = nc.inline_tensor(np.eye(128, dtype=bfdt), "id128b_c").ap()
    id128f = nc.inline_tensor(np.eye(128, dtype=np.float32), "id128f_c").ap()
    id64f = nc.inline_tensor(np.eye(64, dtype=np.float32), "id64f_c").ap()
    mask_np = np.where(np.arange(Q)[None, :] >= np.arange(Q)[:, None], 0.0, -1e30).astype(np.float32)
    mask_c = nc.inline_tensor(mask_np, "mask_c").ap()

    dbg_aps = {}

    def dbg_t(name, shape, dt=F32):
        dbg_aps[name] = nc.dram_tensor("dbg_" + name, shape, dt, kind="ExternalOutput").ap()
        return dbg_aps[name]

    S = Sems(nc)
    S.new(*ENGS)
    S.new("sldx0", "sldx1", "spx0", "spx1", "gldx0", "gldx1", "sp2x0", "sp2x1", "pwx0", "pwx1", "xnld", "sbld", "ginld")
    _sal = {}

    def salloc(name, shape, dt, side="left"):
        cm = nc.sbuf_tensor(name, shape, dt, side=side)
        t = cm.__enter__()
        _sal[name] = cm
        return t

    def sfree(*names):
        for n in names:
            _sal.pop(n).__exit__(None, None, None)
    sy, gp, te, ve, sc = nc.sync, nc.gpsimd, nc.tensor, nc.vector, nc.scalar
    ALL_E = (sy, gp, te, ve, sc)

    p_ta = nc.alloc_psum_tensor("p_ta", [128, 1024], BF16)
    p_tb = nc.alloc_psum_tensor("p_tb", [128, 1024], BF16)
    P = [nc.alloc_psum_tensor("p_%d" % i, [128, 512], F32) for i in range(6)]

    ident_b = salloc("ident_b", [128, 128], BF16)
    ident_f = salloc("ident_f", [128, 128], F32)
    ident64 = salloc("ident64", [64, 64], F32)
    mask_sb = salloc("mask_sb", [128, 128], F32)
    lnw_sb = salloc("lnw_sb", [128, NKD], F32)
    mupc_sb = salloc("mupc_sb", [128, NMT], F32)
    cwf_sb = salloc("cwf_sb", [128, NMX * KC], F32)
    prm_sb = salloc("prm_sb", [64, 8], F32)
    dbc_sb = salloc("dbc_sb", [128, NH], F32)
    nwcol_sb = salloc("nwcol_sb", [128, NKD], F32)
    ones1f = salloc("ones1f", [1, 128], F32)
    onesq64 = salloc("onesq64", [64, Q], F32)
    flag_sb = salloc("flag_sb", [128, 1], F32)
    nonce_sb = salloc("nonce_sb", [1, 16], I32)
    pollbuf = salloc("pollbuf", [1, 16], I32)
    stat = salloc("stat", [128, 8], F32)

    lnw_t = salloc("lnw_t", [32, 128], F32)
    mup_t = salloc("mup_t", [67, 128], F32)
    nw_t = salloc("nw_t", [32, 128], F32)

    S.inc(sy.dma_start(out=ident_b[:], in_=id128b), "sld", 16)
    S.inc(sy.dma_start(out=ident_f[:], in_=id128f), "sld", 16)
    S.inc(sy.dma_start(out=ident64[:], in_=id64f), "sld", 16)
    S.inc(sy.dma_start(out=mask_sb[:], in_=mask_c), "sld", 16)
    S.inc(sy.dma_start(out=lnw_t[:], in_=ln_w.rearrange("(k p) -> k p", p=128)), "sld", 16)
    S.inc(ve.memset(mup_t[64:67, :], 0.0), "dv")
    S.wait(sy, "dv")
    S.inc(sy.dma_start(out=mup_t[0:66, :], in_=mup[0:66 * 128].rearrange("(m p) -> m p", p=128)), "sld", 16)
    S.inc(sy.dma_start(out=mup_t[66:67, 0:64], in_=mup[66 * 128:PROJ].rearrange("(m p) -> m p", p=64)), "sld", 16)
    S.inc(sy.dma_start(out=prm_sb[:, 0:1], in_=a_log.rearrange("(p o) -> p o", o=1)), "sld", 16)
    S.inc(sy.dma_start(out=prm_sb[:, 1:2], in_=dt_bias.rearrange("(p o) -> p o", o=1)), "sld", 16)
    S.inc(sy.dma_start(out=prm_sb[:, 2:3], in_=d_vec.rearrange("(p o) -> p o", o=1)), "sld", 16)
    S.inc(sy.dma_start(out=flag_sb[:], in_=flagv), "sld", 16)
    S.inc(sy.dma_start(out=nonce_sb[:], in_=nonce), "sld", 16)
    for i in range(4):
        mm0, mm1 = i * 9, min(NMX, i * 9 + 9)
        S.inc(sy.dma_start(out=cwf_sb[:, mm0 * KC:mm1 * KC].rearrange("p (m j) -> p m j", j=KC),
                           in_=conv_w[mm0 * 128:mm1 * 128, :].rearrange("(m p) j -> p m j", p=128)), "sld", 16)
    drow = salloc("drow", [1, 64], F32)
    S.inc(sy.dma_start(out=nw_t[:], in_=norm_w.rearrange("(k p) -> k p", p=128)), "sld", 16)
    S.mark("setup_ld", "sld")

    S.wmark(ve, "setup_ld")
    S.inc(ve.memset(ones1f[:], 1.0), "dv")
    S.wait(ve, "dv")
    S.inc(ve.memset(onesq64[:], 1.0), "dv")
    S.mark("ones", "dv")

    S.wmark(sc, "setup_ld")
    S.inc(sc.activation(prm_sb[:, 3:4], prm_sb[:, 0:1], AF.Exp), "ac")
    S.wait(sc, "ac")
    S.inc(sc.mul(prm_sb[:, 3:4], prm_sb[:, 3:4], -1.0), "ac")
    S.mark("A_ready", "ac")

    S.wmark(te, "A_ready")
    S.inc(te.transpose(P[0][0:1, 0:64], prm_sb[:, 2:3], ident64[:, :]), "pe")
    S.wait(ve, "pe")
    S.inc(ve.tensor_copy(drow[:], P[0][0:1, 0:64]), "dv")
    S.wait(te, "dv")
    S.wmark(te, "ones")
    S.inc(te.matmul(P[0][:, 0:64], ones1f[:, :], drow[:, :], start=True, stop=True), "pe")
    S.wait(ve, "pe")
    S.inc(ve.tensor_copy(dbc_sb[:], P[0][:, 0:64]), "dv")

    S.wmark(te, "setup_ld")
    S.wait(te, "dv")
    S.inc(te.transpose(P[1][0:128, 0:32], lnw_t[:, :], ident_f[0:32, 0:32]), "pe")
    S.mark("lnwT_ps", "pe")
    S.inc(te.transpose(P[2][0:128, 0:67], mup_t[:, :], ident_f[0:67, 0:67]), "pe")
    S.mark("mupT_ps", "pe")
    S.wmark(ve, "lnwT_ps")
    S.inc(ve.tensor_copy(lnw_sb[:, :], P[1][0:128, 0:32]), "dv")
    S.wmark(ve, "mupT_ps")
    S.inc(ve.tensor_copy(mupc_sb[:, :], P[2][0:128, 0:67]), "dv")
    S.wait(te, "dv")
    S.inc(te.transpose(P[1][0:128, 0:32], nw_t[:, :], ident_f[0:32, 0:32]), "pe")
    S.mark("nwT_ps", "pe")
    S.wmark(ve, "nwT_ps")
    S.inc(ve.tensor_copy(nwcol_sb[:, :], P[1][0:128, 0:32]), "dv")
    S.mark("lnw_mup", "dv")

    sfree("drow", "nw_t", "mup_t", "lnw_t")

    if stop_after == "setup":
        S.wait(sy, "sld")
        S.barrier(ALL_E)
        return nc, dbg_aps
    dts_sb = salloc("dts_sb", [64, T], F32)
    logdA_sb = salloc("logdA_sb", [64, T], F32)
    logdt_sb = salloc("logdt_sb", [64, T], F32)
    bmT_sb = salloc("bmT_sb", [128, T], BF16)
    cmT_sb = salloc("cmT_sb", [128, T], BF16)
    dtT_sb = salloc("dtT_sb", [64, TT], F32)
    rrb_sb = salloc("rrb_sb", [128, TT], F32)
    rmsxT = salloc("rmsxT", [128, NKD * TT], BF16)
    rrow = salloc("rrow", [1, 128], F32)
    xsq = salloc("xsq", [128, DM], F32)
    xtile = salloc("xtile", [128, DM], F32)
    xtile2 = salloc("xtile2", [128, DM], F32)
    xnb = salloc("xnb", [128, DM], BF16)
    xstg = [salloc("xstg%d" % i, [128, 128], BF16) for i in range(2)]

    import os as _os
    _n05 = int(_os.environ.get("PH05_TILES", "9"))
    for i in range(_n05):
        xt = xtile if i % 2 == 0 else xtile2
        S.wmark_if(sy, "xtdone_%d" % (i - 2))
        S.inc(sy.dma_start(out=xt[:], in_=x_pad[i * 128:(i + 1) * 128, :]), "sldx%d" % (i % 2), 16)
        S.wait(ve, "sldx%d" % (i % 2))
        S.wait(ve, "dv")
        S.wmark_if(ve, "rrbcast_%d" % (i - 1))
        S.inc(ve.tensor_reduce(stat[:, 0:1], xt[:], AX.X, ALU.add, negate=True), "dv")
        S.wait(ve, "dv")
        S.inc(ve.tensor_scalar(stat[:, 0:1], stat[:, 0:1], 1.0 / DM, None, ALU.mult), "dv")
        S.wait(ve, "dv")
        S.inc(ve.tensor_scalar(xt[:], xt[:], stat[:, 0:1], None, ALU.add), "dv")
        S.wait(sc, "dv")
        S.wmark_if(sc, "xsqdone_%d" % (i - 1))
        S.inc(sc.activation(xsq[:], xt[:], AF.Square, accum_out=stat[:, 1:2]), "ac")
        S.mark("xsqdone_%d" % i, "ac")
        S.wait(ve, "ac")
        S.inc(ve.tensor_scalar(stat[:, 1:2], stat[:, 1:2], 1.0 / DM, EPS, ALU.mult, ALU.add), "dv")
        S.wait(sc, "dv")
        S.inc(sc.activation(stat[:, 2:3], stat[:, 1:2], AF.Sqrt), "ac")
        S.wait(ve, "ac")
        S.inc(ve.reciprocal(stat[:, 3:4], stat[:, 2:3]), "dv")
        S.wait(ve, "dv")
        S.inc(ve.tensor_scalar(xnb[:], xt[:], stat[:, 3:4], None, ALU.mult), "dv")
        S.mark("xtdone_%d" % i, "dv")
        S.wait(ve, "dv")
        S.inc(ve.tensor_scalar(stat[:, 4:5], stat[:, 1:2], EPS, None, ALU.subtract), "dv")
        S.wait(ve, "dv")
        S.inc(ve.tensor_tensor(stat[:, 5:6], stat[:, 3:4], stat[:, 3:4], ALU.mult), "dv")
        S.wait(ve, "dv")
        S.inc(ve.tensor_tensor(stat[:, 4:5], stat[:, 4:5], stat[:, 5:6], ALU.mult), "dv")
        S.wait(ve, "dv")
        S.inc(ve.tensor_scalar(stat[:, 4:5], stat[:, 4:5], EPS, None, ALU.add), "dv")
        S.wait(sc, "dv")
        S.inc(sc.activation(stat[:, 5:6], stat[:, 4:5], AF.Sqrt), "ac")
        S.wait(ve, "ac")
        S.inc(ve.reciprocal(stat[:, 6:7], stat[:, 5:6]), "dv")
        S.wait(te, "dv")
        S.wmark_if(te, "rrbcast_%d" % (i - 1))
        S.inc(te.transpose(P[0][0:1, 0:128], stat[:, 6:7], ident_f[:, :]), "pe")
        S.wait(ve, "pe")
        S.inc(ve.tensor_copy(rrow[:], P[0][0:1, 0:128]), "dv")
        S.wait(te, "dv")
        S.inc(te.matmul(P[0][:, 0:128], ones1f[:, :], rrow[:, :], start=True, stop=True), "pe")
        S.wait(ve, "pe")
        S.inc(ve.tensor_copy(rrb_sb[:, i * 128:(i + 1) * 128], P[0][:, 0:128]), "dv")
        S.mark("rrbcast_%d" % i, "dv")
        for k in range(NKD):
            pt = p_ta if k % 2 == 0 else p_tb
            S.wmark_if(te, "ptf_%d" % (k % 2))
            S.wmark_if(te, "ptv_%d" % (k % 2))
            S.wait(te, "dv")
            S.inc(te.transpose(pt[:, 0:128], xnb[:, k * 128:(k + 1) * 128], ident_b[:, :]), "pe")
            if i >= 1:
                stg = xstg[k % 2]
                S.wmark_if(sc, "xstgf_%d" % (k % 2))
                S.wmark_if(sc, "ptv_%d" % (k % 2))
                S.wait(sc, "pe")
                S.inc(sc.copy(stg[:, :], pt[:, 0:128]), "ac")
                S.mark("ptf_%d" % (k % 2), "ac")
                S.wait(ve, "ac")
                S.inc(ve.tensor_scalar(rmsxT[:, k * TT + i * 128:k * TT + (i + 1) * 128], stg[:, :], lnw_sb[:, k:k + 1], None, ALU.mult), "dv")
                S.mark("ptv_%d" % (k % 2), "dv")
                S.wait(gp, "dv")
                S.inc(gp.dma_start(out=xnT_sp[k, :, (i - 1) * 128:i * 128], in_=stg[:, :]), "spx%d" % (k % 2), 16)
                S.mark("xstgf_%d" % (k % 2), "spx%d" % (k % 2))
            else:
                S.wait(ve, "pe")
                S.inc(ve.tensor_scalar(rmsxT[:, k * TT + i * 128:k * TT + (i + 1) * 128], pt[:, 0:128], lnw_sb[:, k:k + 1], None, ALU.mult), "dv")
                S.mark("ptv_%d" % (k % 2), "dv")
                S.mark("ptf_%d" % (k % 2), "dv")
    S.mark("ph05_done", "dv")

    if stop_after == "ph05":
        S.wait(sy, "sp")
        S.barrier(ALL_E)
        return nc, dbg_aps
    sfree("xstg1", "xstg0", "xnb", "xtile2", "xtile", "xsq", "rrow")
    S.barrier(ALL_E)

    wm = [salloc("wm%d" % i, [128, NKD * 128], BF16) for i in range(2)]
    xbcT = [salloc("xbcT%d" % i, [128, TT], F32) for i in range(2)]
    cacc = salloc("cacc", [128, T], F32)
    zstage = salloc("zstage", [128, T], BF16)
    xsstage = salloc("xsstage", [128, T], BF16)

    S.wmark(gp, "setup_ld")
    for m in range(NMT):
        wmb = wm[m % 2]
        is_z = m < NMZ
        is_dt = m == NMT - 1
        mm = m - NMZ
        ncols = 64 if is_dt else 128
        S.wmark_if(gp, "wmfree_%d" % (m % 2))
        for kk in range(4):
            S.inc(gp.dma_start(
                out=wmb[:, kk * 8 * ncols:(kk + 1) * 8 * ncols].rearrange("p (k c) -> p k c", k=8),
                in_=w_in[kk * 8 * 128:(kk + 1) * 8 * 128, m * 128:m * 128 + ncols].rearrange("(k p) c -> p k c", p=128)),
                "gldx%d" % (m % 2), 16)
        S.mark("wld_%d" % m, "gldx%d" % (m % 2))
        tcs = [(128, 512), (640, 512)] if is_z else [(0, 512), (512, 512), (1024, 128)]
        if m == 0:
            S.wmark(te, "ph05_done")
        for ci, (t0, tn) in enumerate(tcs):
            pj = P[ci % 3]
            S.wmark(te, "wld_%d" % m)
            S.wmark_if(te, "pjev_%d" % (ci % 3))
            for k in range(NKD):
                ins = te.matmul(pj[0:ncols, 0:tn],
                                wmb[:, k * ncols:(k + 1) * ncols],
                                rmsxT[:, k * TT + t0:k * TT + t0 + tn],
                                start=(k == 0), stop=(k == NKD - 1))
            S.inc(ins, "pe")
            S.mark("pjmm_%d" % (ci % 3), "pe")
            S.wmark(ve, "pjmm_%d" % (ci % 3))
            S.wait(ve, "dv")
            if is_z:
                S.wmark_if(ve, "zspill")
                S.inc(ve.scalar_tensor_tensor(zstage[:, t0 - ET:t0 - ET + tn], pj[0:128, 0:tn], mupc_sb[:, m:m + 1], rrb_sb[:, t0:t0 + tn], ALU.mult, ALU.mult), "dv")
            elif is_dt:
                S.inc(ve.scalar_tensor_tensor(dtT_sb[0:64, t0:t0 + tn], pj[0:64, 0:tn], mupc_sb[0:64, m:m + 1], rrb_sb[0:64, t0:t0 + tn], ALU.mult, ALU.mult), "dv")
            else:
                S.inc(ve.scalar_tensor_tensor(xbcT[m % 2][:, t0:t0 + tn], pj[0:128, 0:tn], mupc_sb[:, m:m + 1], rrb_sb[:, t0:t0 + tn], ALU.mult, ALU.mult), "dv")
            S.mark("pjev_%d" % (ci % 3), "dv")
        S.marks["wmfree_%d" % (m % 2)] = ("pe", S.c["pe"])

        if is_z:
            S.wait(sy, "dv")
            S.inc(sy.dma_start(out=zT_sp[m, :, :], in_=zstage[:, :]), "sp", 16)
            S.mark("zspill", "sp")
        elif not is_dt:
            xb = xbcT[m % 2]
            S.wait(ve, "dv")
            S.wait(ve, "ac")
            S.inc(ve.tensor_scalar(cacc[:], xb[:, 125:125 + T], cwf_sb[:, mm * KC:mm * KC + 1], None, ALU.mult), "dv")
            for j in range(1, KC):
                S.wait(ve, "dv")
                S.inc(ve.scalar_tensor_tensor(cacc[:], xb[:, 125 + j:125 + j + T], cwf_sb[:, mm * KC + j:mm * KC + j + 1], cacc[:], ALU.mult, ALU.add), "dv")
            S.wait(sc, "dv")
            if mm < 32:
                S.wmark_if(sc, "xsspill")
                S.inc(sc.activation(xsstage[:, :], cacc[:], AF.Silu), "ac")
                S.wait(sy, "ac")
                S.inc(sy.dma_start(out=xsT_sp[mm, :, :], in_=xsstage[:, :]), "sp", 16)
                S.mark("xsspill", "sp")
            elif mm == 32:
                S.inc(sc.activation(bmT_sb[:, :], cacc[:], AF.Silu), "ac")
            else:
                S.inc(sc.activation(cmT_sb[:, :], cacc[:], AF.Silu), "ac")

    S.wait(ve, "dv")
    S.inc(ve.tensor_scalar(dtT_sb[0:64, ET:TT], dtT_sb[0:64, ET:TT], prm_sb[:, 1:2], None, ALU.add), "dv")
    S.wait(sc, "dv")
    S.inc(sc.activation(logdA_sb[0:64, :], dtT_sb[0:64, ET:TT], AF.Abs), "ac")
    S.wait(sc, "ac")
    S.inc(sc.activation(logdt_sb[0:64, :], logdA_sb[0:64, :], AF.Exp, scale=-1.0), "ac")
    S.wait(sc, "ac")
    S.inc(sc.activation(logdt_sb[0:64, :], logdt_sb[0:64, :], AF.Ln, bias=1.0), "ac")
    S.wait(ve, "ac")
    S.inc(ve.tensor_scalar(logdA_sb[0:64, :], dtT_sb[0:64, ET:TT], 0.0, None, ALU.max), "dv")
    S.wait(ve, "dv")
    S.inc(ve.tensor_tensor(dts_sb[0:64, :], logdA_sb[0:64, :], logdt_sb[0:64, :], ALU.add), "dv")
    S.wmark(ve, "A_ready")
    S.wait(ve, "dv")
    S.inc(ve.tensor_scalar(logdA_sb[0:64, :], dts_sb[0:64, :], prm_sb[:, 3:4], None, ALU.mult), "dv")
    S.wait(sc, "dv")
    S.inc(sc.activation(logdt_sb[0:64, :], dts_sb[0:64, :], AF.Ln), "ac")
    S.mark("dt_done", "ac")
    S.mark("ph1_done", "dv")

    if "zT" in dbg:
        d = dbg_t("zT", [NMZ, 128, T], BF16)
        S.wait(sy, "sp")
        S.inc(sy.dma_start(out=d, in_=zT_sp), "sp", 16)
    if "xsT" in dbg:
        d = dbg_t("xsT", [NMZ, 128, T], BF16)
        S.wait(sy, "sp")
        S.inc(sy.dma_start(out=d, in_=xsT_sp), "sp", 16)
    if "xnT" in dbg:
        d = dbg_t("xnT", [NKD, 128, T], BF16)
        S.wait(sy, "sp")
        S.inc(sy.dma_start(out=d, in_=xnT_sp), "sp", 16)
    if "dts" in dbg:
        d = dbg_t("dts", [64, T])
        S.wmark(sy, "dt_done")
        S.wmark(sy, "ph1_done")
        S.inc(sy.dma_start(out=d, in_=dts_sb[0:64, :]), "sp", 16)
    if "bcT" in dbg:
        d = dbg_t("bcT", [256, T], BF16)
        S.wmark(sy, "ph1_done")
        S.wait(sy, "ac")
        S.inc(sy.dma_start(out=d[0:128, :], in_=bmT_sb[:, :]), "sp", 16)
        S.inc(sy.dma_start(out=d[128:256, :], in_=cmT_sb[:, :]), "sp", 16)

    sfree("xsstage", "zstage", "cacc", "xbcT1", "xbcT0", "wm1", "wm0", "rmsxT", "rrb_sb", "dtT_sb")
    S.barrier(ALL_E)

    if stop_after == "ph1":
        S.wait(sy, "sp")
        S.wait(sy, "ac")
        return nc, dbg_aps
    y_acc = salloc("y_acc", [128, NQ * DM], BF16)
    hfp = salloc("hfp", [128, NH * HD], F32)
    hbf = salloc("hbf", [128, NH * HD], BF16)
    hg = salloc("hg", [128, NH * HD], BF16)
    xstq = [salloc("xstq%d" % i, [128, 32 * Q], BF16) for i in range(2)]
    xs_nat = salloc("xs_nat", [128, DM], BF16)
    bnat = salloc("bnat", [128, 128], BF16)
    cumc = salloc("cumc", [64, Q], F32)
    cumT = salloc("cumT", [128, 64], F32)
    ldtT = salloc("ldtT", [128, 64], F32)
    gT = salloc("gT", [128, 64], F32)
    vT_all = salloc("vT_all", [128, NQ * 64], F32)
    ecq_all = salloc("ecq_all", [128, NQ * 64], F32)
    gamb = salloc("gamb", [128, 64], F32)
    sTq = salloc("sTq", [128, 64], F32)
    gts = salloc("gts", [128, 128], F32)
    me_sb = salloc("me_sb", [128, 128], F32)
    mt_sb = [salloc("mt_sb%d" % i, [128, 128], BF16) for i in range(2)]
    xsw = [salloc("xsw%d" % i, [128, 64], BF16) for i in range(2)]
    crow = salloc("crow", [1, 128], F32)

    S.wait(ve, "dv")
    S.inc(ve.memset(hfp[:], 0.0), "dv")
    S.wait(ve, "dv")
    S.inc(ve.memset(hbf[:], 0.0), "dv")
    S.mark("hbf_init", "dv")
    S.marks["hbf_-1"] = S.marks["hbf_init"]

    for q in range(NQ):
        qs = slice(q * Q, (q + 1) * Q)
        xq = xstq[q % 2]
        S.wmark_if(sy, "xstq_free_%d" % (q % 2))
        if q == 0:
            S.wait(sy, "sp")
        for a in range(4):
            S.inc(sy.dma_start(out=xq[:, a * 8 * Q:(a + 1) * 8 * Q].rearrange("p (m t) -> p m t", m=8),
                               in_=xsT_sp[a * 8:(a + 1) * 8, :, qs].rearrange("m p t -> p m t")), "sp2x%d" % (q % 2), 16)
        S.mark("xsld_%d" % q, "sp2x%d" % (q % 2))

        if q == 0:
            S.wmark(ve, "dt_done")
            S.wmark(ve, "ph1_done")
        S.wmark_if(ve, "cum_cons_%d" % (q - 1))
        S.wait(ve, "dv")
        S.inc(ve.tensor_tensor_scan(cumc[:, :], onesq64[:, :], logdA_sb[0:64, qs], 0.0, ALU.mult, ALU.add), "dv")
        S.mark("cum_%d" % q, "dv")
        S.wmark(te, "cum_%d" % q)
        S.wmark_if(te, "p0_free")
        S.inc(te.transpose(P[0][0:128, 0:64], cumc[:, :], ident64[:, :]), "pe")
        S.mark("cumT_ps", "pe")
        S.wmark_if(te, "p1_free")
        S.inc(te.transpose(P[1][0:128, 0:64], logdt_sb[0:64, qs], ident64[:, :]), "pe")
        S.mark("ldtT_ps", "pe")
        S.wmark(ve, "cumT_ps")
        S.wmark_if(ve, "cumT_cons_%d" % (q - 1))
        S.inc(ve.tensor_copy(cumT[:, :], P[0][0:128, 0:64]), "dv")
        S.mark("cumT_%d" % q, "dv")
        S.wmark(ve, "ldtT_ps")
        S.inc(ve.tensor_copy(ldtT[:, :], P[1][0:128, 0:64]), "dv")
        S.mark("ldtT_%d" % q, "dv")
        S.wmark(sc, "cumT_%d" % q)
        S.inc(sc.activation(vT_all[:, q * 64:(q + 1) * 64], cumT[:, :], AF.Exp), "ac")
        S.mark("vT_%d" % q, "ac")
        S.wait(ve, "dv")
        S.inc(ve.tensor_reduce(stat[0:64, 7:8], logdA_sb[0:64, qs], AX.X, ALU.add), "dv")
        S.mark("cq_red_%d" % q, "dv")
        S.wmark(te, "cq_red_%d" % q)
        S.inc(te.transpose(P[2][0:1, 0:64], stat[0:64, 7:8], ident64[:, :]), "pe")
        S.mark("cqrow_ps", "pe")
        S.wmark(ve, "cqrow_ps")
        S.inc(ve.tensor_copy(crow[:, 0:64], P[2][0:1, 0:64]), "dv")
        S.mark("cqrow_%d" % q, "dv")
        S.wmark(te, "cqrow_%d" % q)
        S.inc(te.matmul(P[2][:, 0:64], ones1f[:, :], crow[:, 0:64], start=True, stop=True), "pe")
        S.mark("cqb_ps", "pe")
        S.wmark(ve, "cqb_ps")
        S.wmark(ve, "ldtT_%d" % q)
        S.inc(ve.tensor_tensor(gT[:, :], P[2][:, 0:64], cumT[:, :], ALU.subtract), "dv")
        S.mark("cqbread_%d" % q, "dv")
        S.wmark(sc, "cqbread_%d" % q)
        S.inc(sc.activation(ecq_all[:, q * 64:(q + 1) * 64], P[2][:, 0:64], AF.Exp), "ac")
        S.mark("ecq_%d" % q, "ac")
        S.wait(ve, "dv")
        S.inc(ve.tensor_tensor(gT[:, :], gT[:, :], ldtT[:, :], ALU.add), "dv")
        S.mark("gt_pre", "dv")
        S.wmark(sc, "gt_pre")
        S.wmark(sc, "ecq_%d" % q)
        S.inc(sc.activation(gT[:, :], gT[:, :], AF.Exp), "ac")
        S.mark("gT_%d" % q, "ac")

        S.wmark_if(te, "bnat_cons_%d" % (q - 1))
        if q == 0:
            S.wmark(te, "ph1_done")
            S.wait(te, "ac")
        S.inc(te.transpose(p_ta[:, 0:128], bmT_sb[:, qs], ident_b[:, :]), "pe")
        S.mark("bnat_ps", "pe")
        S.wmark(ve, "bnat_ps")
        S.inc(ve.tensor_copy(bnat[:, :], p_ta[:, 0:128]), "dv")
        S.mark("bnat_%d" % q, "dv")
        S.wmark_if(te, "gts_cons_%d" % (q - 1))
        S.inc(te.matmul(P[3][:, 0:128], bmT_sb[:, qs], cmT_sb[:, qs], start=True, stop=True), "pe")
        S.mark("gts_ps", "pe")
        S.wmark(ve, "gts_ps")
        S.inc(ve.tensor_copy(gts[:, :], P[3][:, 0:128]), "dv")
        S.mark("gts_%d" % q, "dv")

        S.wmark(te, "xsld_%d" % q)
        S.wmark(te, "bnat_%d" % q)
        S.wmark_if(te, "xsnat_cons_%d" % (q - 1))
        for k in range(NKD):
            pt = p_ta if k % 2 == 0 else p_tb
            S.wmark_if(te, "xsnat_ev_%d" % (k - 2 if k >= 2 else k + 30))
            S.inc(te.transpose(pt[:, 0:128], xq[:, k * Q:(k + 1) * Q], ident_b[:, :]), "pe")
            S.mark("xsnat_ps_%d" % k, "pe")
            S.wmark(ve, "xsnat_ps_%d" % k)
            S.inc(ve.tensor_copy(xs_nat[:, k * 128:(k + 1) * 128], pt[:, 0:128]), "dv")
            S.mark("xsnat_ev_%d" % k, "dv")
        S.mark("xsnat_%d" % q, "dv")
        S.mark("xstq_free_%d" % (q % 2), "dv")

        for grp in range(8):
            py = P[4 + grp % 2]
            S.wmark(te, "hbf_%d" % (q - 1))
            S.wmark_if(te, "yiev_%d" % (grp % 2))
            S.inc(te.matmul(py[:, 0:512], cmT_sb[:, qs], hbf[:, grp * 512:(grp + 1) * 512], start=True, stop=True), "pe")
            S.mark("yimm_%d" % (grp % 2), "pe")
            S.wmark(ve, "yimm_%d" % (grp % 2))
            S.wmark(ve, "vT_%d" % q)
            for rr in range(8):
                r = grp * 8 + rr
                ysl = y_acc[:, q * DM + r * 64:q * DM + (r + 1) * 64]
                S.wait(ve, "dv")
                S.inc(ve.tensor_scalar(ysl, py[:, rr * 64:(rr + 1) * 64], vT_all[:, q * 64 + r:q * 64 + r + 1], None, ALU.mult), "dv")
            S.mark("yiev_%d" % (grp % 2), "dv")
        S.mark("yinter_%d" % q, "dv")
        S.mark("p45_free", "dv")

        S.wmark(te, "xsnat_%d" % q)
        for r in range(NH):
            pcb = P[r % 2]
            py2 = P[2 + r % 2]
            mt = mt_sb[r % 2]
            xw = xsw[r % 2]
            S.wmark_if(te, "me_%d" % (r - 2))
            if r == 0:
                S.mark("p0_free", "pe")
                S.wmark(te, "cumT_%d" % q)
            if r <= 1:
                S.wmark_if(te, "crow_cons_prev")
            S.wmark_if(te, "crow_%d" % (r - 1))
            S.inc(te.transpose(pcb[0:1, 0:128], cumT[:, r:r + 1], ident_f[:, :]), "pe")
            S.mark("crow_ps", "pe")
            S.wmark(ve, "crow_ps")
            S.wmark_if(ve, "crow_cons_%d" % (r - 1))
            S.inc(ve.tensor_copy(crow[:, :], pcb[0:1, 0:128]), "dv")
            S.mark("crow_%d" % r, "dv")
            S.wmark(te, "crow_%d" % r)
            S.inc(te.matmul(pcb[:, 0:128], ones1f[:, :], crow[:, :], start=True, stop=True), "pe")
            S.mark("cb_%d" % r, "pe")
            S.mark("crow_cons_%d" % r, "pe")
            S.wmark(ve, "cb_%d" % r)
            S.wmark_if(ve, "mt_cons_%d" % (r - 2))
            S.wait(ve, "dv")
            S.inc(ve.tensor_scalar(me_sb[:, :], pcb[:, 0:128], cumT[:, r:r + 1], ldtT[:, r:r + 1], ALU.subtract, ALU.add), "dv")
            S.mark("me_%d" % r, "dv")
            S.wait(ve, "dv")
            S.inc(ve.tensor_tensor(me_sb[:, :], me_sb[:, :], mask_sb[:, :], ALU.add), "dv")
            S.wait(sc, "dv")
            S.inc(sc.activation(me_sb[:, :], me_sb[:, :], AF.Exp), "ac")
            S.wait(ve, "ac")
            S.wmark(ve, "gts_%d" % q)
            S.inc(ve.tensor_tensor(mt[:, :], me_sb[:, :], gts[:, :], ALU.mult), "dv")
            S.mark("mt_%d" % r, "dv")
            S.wmark(te, "mt_%d" % r)
            S.wmark_if(te, "y2ev_%d" % (r - 2))
            S.wmark_if(te, "hup_%d" % (r - 2))
            S.inc(te.matmul(py2[:, 0:64], mt[:, :], xs_nat[:, r * 64:(r + 1) * 64], start=True, stop=True), "pe")
            S.mark("y2_%d" % r, "pe")
            S.mark("mt_cons_%d" % r, "pe")
            ysl = y_acc[:, q * DM + r * 64:q * DM + (r + 1) * 64]
            S.wmark(ve, "y2_%d" % r)
            S.wmark(ve, "yinter_%d" % q)
            S.wait(ve, "dv")
            S.inc(ve.tensor_tensor(ysl, ysl, py2[:, 0:64], ALU.add), "dv")
            S.wait(ve, "dv")
            S.inc(ve.scalar_tensor_tensor(ysl, xs_nat[:, r * 64:(r + 1) * 64], dbc_sb[:, r:r + 1], ysl, ALU.mult, ALU.add), "dv")
            S.mark("y2ev_%d" % r, "dv")
            S.wmark(ve, "gT_%d" % q)
            S.wmark_if(ve, "xsw_cons_%d" % (r - 2))
            S.wait(ve, "dv")
            S.inc(ve.tensor_scalar(xw[:, :], xs_nat[:, r * 64:(r + 1) * 64], gT[:, r:r + 1], None, ALU.mult), "dv")
            S.mark("xsw_%d" % r, "dv")
            S.wmark(te, "xsw_%d" % r)
            S.wmark(te, "bnat_%d" % q)
            S.wmark(te, "y2ev_%d" % r)
            S.inc(te.matmul(py2[:, 64:128], bnat[:, :], xw[:, :], start=True, stop=True), "pe")
            S.mark("sprime_%d" % r, "pe")
            S.mark("xsw_cons_%d" % r, "pe")
            S.wmark(ve, "sprime_%d" % r)
            S.wmark(ve, "ecq_%d" % q)
            S.wait(ve, "dv")
            S.inc(ve.scalar_tensor_tensor(hfp[:, r * 64:(r + 1) * 64], hfp[:, r * 64:(r + 1) * 64], ecq_all[:, q * 64 + r:q * 64 + r + 1], py2[:, 64:128], ALU.mult, ALU.add), "dv")
            S.wait(ve, "dv")
            S.inc(ve.tensor_copy(hbf[:, r * 64:(r + 1) * 64], hfp[:, r * 64:(r + 1) * 64]), "dv")
            S.mark("hup_%d" % r, "dv")
        S.mark("hbf_%d" % q, "dv")
        S.mark("cum_cons_%d" % q, "pe")
        S.mark("cumT_cons_%d" % q, "pe")
        S.mark("bnat_cons_%d" % q, "pe")
        S.mark("gts_cons_%d" % q, "pe")
        S.mark("xsnat_cons_%d" % q, "pe")
        S.marks["p1_free"] = ("dv", S.c["dv"])

    cid = sy.partition_id()
    mine = cid % 2
    other = (cid + 1) % 2
    S.wmark(sy, "hbf_%d" % (NQ - 1))
    S.inc(sy.dma_start(out=shH[bass.ds(mine * 128, 128), :], in_=hbf[:, :]), "hsh", 16)
    S.wait(sy, "hsh")
    S.inc(sy.dma_start(out=shF[bass.ds(mine, 1), :], in_=nonce_sb[:, :]), "hsh", 16)
    S.wait(sy, "hsh")
    S.new("pol")
    with sy.register("cont") as cont, sy.register("expv") as expv, sy.register("fv") as fv, sy.register("nv") as nv:
        sy.reg_load(nv, nonce_sb[0:1, 0:1])
        sy.reg_mov(cont, 1)
        sy.reg_mov(expv, 0)
        with sy.While(cont):
            sy.reg_add(expv, expv, 16)
            sy.dma_start(out=pollbuf[:, :], in_=shF[bass.ds(other, 1), :]).then_inc(S.h["pol"], 16)
            sy.wait_ge(S.h["pol"], expv)
            sy.reg_load(fv, pollbuf[0:1, 0:1])
            sy.reg_alu(cont, fv, nv, ALU.not_equal)
    S.inc(sy.dma_start(out=hg[:, :], in_=shH[bass.ds(other * 128, 128), :]), "hsh", 16)
    S.wait(ve, "hsh")
    S.wait(ve, "dv")
    S.inc(ve.tensor_scalar(hg[:, :], hg[:, :], flag_sb[:, 0:1], None, ALU.mult), "dv")
    S.wait(ve, "dv")
    S.inc(ve.memset(gamb[:, :], 1.0), "dv")
    S.mark("hg_ready", "dv")

    for q in range(NQ):
        qs = slice(q * Q, (q + 1) * Q)
        S.wait(ve, "dv")
        S.inc(ve.tensor_tensor(sTq[:, :], vT_all[:, q * 64:(q + 1) * 64], gamb[:, :], ALU.mult), "dv")
        S.mark("sTq_%d" % q, "dv")
        for grp in range(8):
            py = P[4 + grp % 2]
            S.wmark(te, "hg_ready")
            S.wmark_if(te, "dyev_%d" % (grp % 2))
            if q == 0 and grp <= 1:
                S.wmark(te, "p45_free")
            S.inc(te.matmul(py[:, 0:512], cmT_sb[:, qs], hg[:, grp * 512:(grp + 1) * 512], start=True, stop=True), "pe")
            S.mark("dymm_%d" % (grp % 2), "pe")
            S.wmark(ve, "dymm_%d" % (grp % 2))
            S.wmark(ve, "sTq_%d" % q)
            for rr in range(8):
                r = grp * 8 + rr
                ysl = y_acc[:, q * DM + r * 64:q * DM + (r + 1) * 64]
                S.wait(ve, "dv")
                S.inc(ve.scalar_tensor_tensor(ysl, py[:, rr * 64:(rr + 1) * 64], sTq[:, r:r + 1], ysl, ALU.mult, ALU.add), "dv")
            S.mark("dyev_%d" % (grp % 2), "dv")
        S.wait(ve, "dv")
        S.inc(ve.tensor_tensor(gamb[:, :], gamb[:, :], ecq_all[:, q * 64:(q + 1) * 64], ALU.mult), "dv")
    S.mark("y_done", "dv")

    if "y" in dbg:
        d = dbg_t("y", [128, NQ * DM], BF16)
        S.wmark(sy, "y_done")
        S.inc(sy.dma_start(out=d, in_=y_acc[:, :]), "sp", 16)
    if "H" in dbg:
        d = dbg_t("H", [128, NH * HD], BF16)
        S.wmark(sy, "hbf_%d" % (NQ - 1))
        S.inc(sy.dma_start(out=d, in_=hbf[:, :]), "sp", 16)

    sfree("crow", "xsw1", "xsw0", "mt_sb1", "mt_sb0", "me_sb", "gts", "sTq", "gamb",
          "ecq_all", "vT_all", "gT", "ldtT", "cumT", "cumc", "bnat", "xs_nat",
          "xstq1", "xstq0", "hg", "hbf", "hfp")

    if stop_after == "ph2":
        S.wait(sy, "sp")
        S.barrier(ALL_E)
        return nc, dbg_aps
    S.barrier(ALL_E)
    ynT = salloc("ynT", [128, NKD * T], BF16, side="right")
    ztq = [salloc("ztq%d" % i, [128, 32 * Q], BF16) for i in range(2)]
    sz = salloc("sz", [128, DM], BF16)
    yg = salloc("yg", [128, DM], F32)
    ynb = salloc("ynb", [128, DM], BF16)

    for q in range(NQ):
        zq = ztq[q % 2]
        S.wmark_if(sy, "ztq_free_%d" % (q % 2))
        for a in range(4):
            S.inc(sy.dma_start(out=zq[:, a * 8 * Q:(a + 1) * 8 * Q].rearrange("p (m t) -> p m t", m=8),
                               in_=zT_sp[a * 8:(a + 1) * 8, :, q * Q:(q + 1) * Q].rearrange("m p t -> p m t")), "sp2x%d" % (q % 2), 16)
        S.mark("zld_%d" % q, "sp2x%d" % (q % 2))
        S.wmark(te, "zld_%d" % q)
        S.wmark_if(te, "sz_cons_%d" % (q - 1))
        for k in range(NKD):
            pt = p_ta if k % 2 == 0 else p_tb
            S.wmark_if(te, "zsilu_%d" % (k - 2 if k >= 2 else k + 30))
            if k < 2:
                S.wmark_if(te, "ynT_ev_%d" % (k + 30))
            S.inc(te.transpose(pt[:, 0:128], zq[:, k * Q:(k + 1) * Q], ident_b[:, :]), "pe")
            S.mark("zT_ps_%d" % k, "pe")
            S.wmark(sc, "zT_ps_%d" % k)
            S.wait(sc, "ac")
            S.inc(sc.activation(sz[:, k * 128:(k + 1) * 128], pt[:, 0:128], AF.Silu), "ac")
            S.mark("zsilu_%d" % k, "ac")
        S.mark("ztq_free_%d" % (q % 2), "ac")
        S.wait(ve, "ac")
        S.wait(ve, "dv")
        S.inc(ve.tensor_tensor(yg[:, :], y_acc[:, q * DM:(q + 1) * DM], sz[:, :], ALU.mult), "dv")
        S.mark("sz_cons_%d" % q, "dv")
        S.wait(sc, "dv")
        S.inc(sc.activation(sz[:, :], yg[:, :], AF.Square, accum_out=stat[:, 0:1]), "ac")
        S.wait(ve, "ac")
        S.inc(ve.tensor_scalar(stat[:, 0:1], stat[:, 0:1], 1.0 / DM, EPS, ALU.mult, ALU.add), "dv")
        S.wait(sc, "dv")
        S.inc(sc.activation(stat[:, 1:2], stat[:, 0:1], AF.Sqrt), "ac")
        S.wait(ve, "ac")
        S.inc(ve.reciprocal(stat[:, 2:3], stat[:, 1:2]), "dv")
        S.wait(ve, "dv")
        S.inc(ve.tensor_scalar(ynb[:, :], yg[:, :], stat[:, 2:3], None, ALU.mult), "dv")
        S.mark("ynb_%d" % q, "dv")
        S.wmark(te, "ynb_%d" % q)
        for k in range(NKD):
            pt = p_ta if k % 2 == 0 else p_tb
            S.wmark_if(te, "ynT_ev_%d" % (k - 2 if k >= 2 else k + 30))
            S.inc(te.transpose(pt[:, 0:128], ynb[:, k * 128:(k + 1) * 128], ident_b[:, :]), "pe")
            S.mark("ynT_ps_%d" % k, "pe")
            S.wmark(ve, "ynT_ps_%d" % k)
            S.inc(ve.tensor_scalar(ynT[:, k * T + q * Q:k * T + (q + 1) * Q], pt[:, 0:128], nwcol_sb[:, k:k + 1], None, ALU.mult), "dv")
            S.mark("ynT_ev_%d" % k, "dv")
    S.mark("ynT_done", "dv")

    if "ynT" in dbg:
        d = dbg_t("ynT", [128, NKD * T], BF16)
        S.wmark(sy, "ynT_done")
        S.inc(sy.dma_start(out=d, in_=ynT[:, :]), "sp", 16)

    sfree("ynb", "yg", "sz", "ztq1", "ztq0", "y_acc")
    wo = [salloc("wo%d" % i, [128, NKD * 512], BF16) for i in range(2)]
    sstg = salloc("sstg", [128, 512], BF16)
    S.mark("preop_pe", "pe")
    S.mark("preop_ac", "ac")
    for dc in range(8):
        wob = wo[dc % 2]
        if dc == 0:
            S.wmark(gp, "ynT_done")
            S.wmark(gp, "preop_pe")
            S.wmark(gp, "preop_ac")
        S.wmark_if(gp, "wofree_%d" % (dc % 2))
        for kk in range(4):
            S.inc(gp.dma_start(out=wob[:, kk * 8 * 512:(kk + 1) * 8 * 512].rearrange("p (k c) -> p k c", k=8),
                               in_=w_out[kk * 8 * 128:(kk + 1) * 8 * 128, dc * 512:(dc + 1) * 512].rearrange("(k p) c -> p k c", p=128)), "gldx%d" % (dc % 2), 16)
        S.mark("wold_%d" % dc, "gldx%d" % (dc % 2))
        S.wmark(te, "wold_%d" % dc)
        if dc == 0:
            S.wmark(te, "ynT_done")
        for tq in range(NQ):
            py = P[4 + tq % 2]
            S.wmark_if(te, "sev_%d" % (tq % 2))
            for k in range(NKD):
                ins = te.matmul(py[:, 0:512], ynT[:, k * T + tq * Q:k * T + (tq + 1) * Q], wob[:, k * 512:(k + 1) * 512],
                                start=(k == 0), stop=(k == NKD - 1))
            S.inc(ins, "pe")
            S.mark("smm_%d" % (tq % 2), "pe")
            S.wmark(ve, "smm_%d" % (tq % 2))
            S.wmark_if(ve, "sspill")
            S.wait(ve, "dv")
            S.inc(ve.tensor_copy(sstg[:, :], py[:, 0:512]), "dv")
            S.mark("sev_%d" % (tq % 2), "dv")
            S.wait(sy, "dv")
            S.inc(sy.dma_start(out=s_sp[tq, :, dc * 512:(dc + 1) * 512], in_=sstg[:, :]), "sp", 16)
            S.mark("sspill", "sp")
        S.marks["wofree_%d" % (dc % 2)] = ("pe", S.c["pe"])
    S.mark("s_done", "sp")

    if "s" in dbg:
        d = dbg_t("s", [NQ, 128, DM], BF16)
        S.wmark(sy, "s_done")
        S.inc(sy.dma_start(out=d, in_=s_sp), "sp", 16)

    if stop_after == "ph3":
        S.wait(sy, "sp")
        S.barrier(ALL_E)
        return nc, dbg_aps
    S.barrier(ALL_E)
    sfree("sstg", "wo1", "wo0", "ynT", "cmT_sb", "bmT_sb", "logdt_sb", "logdA_sb", "dts_sb")
    xnT = salloc("xnT", [128, NKD * T], BF16)
    hact = salloc("hact", [128, 29 * T], BF16)
    s2all = salloc("s2all", [128, 32 * 172], F32)
    s1b = salloc("s1b", [128, GRP], F32)
    s3b = salloc("s3b", [128, GRP], F32)
    pw_i = [salloc("pw_i%d" % i, [128, 8 * GRP], I32) for i in range(2)]
    pw_s1 = salloc("pw_s1", [128, 8 * GRP], I32)
    dq_big = [salloc("dq_big%d" % i, [128, 1024], BF16) for i in range(2)]
    w1t = [salloc("w1t%d" % i, [128, 1024], BF16) for i in range(4)]
    w2t_dc = salloc("w2t_dc", [128, 29 * 512], BF16)
    silu_t = salloc("silu_t", [128, 512], F32)
    gin = salloc("gin", [128, 512], F32)
    sstg2 = salloc("sstg2", [128, 512], BF16)
    xstg2 = salloc("xstg2", [128, 512], F32)
    ostg = salloc("ostg", [128, 512], F32)

    for a in range(8):
        S.inc(sy.dma_start(out=xnT[:, a * 4 * T:(a + 1) * 4 * T].rearrange("p (m t) -> p m t", m=4),
                           in_=xnT_sp[a * 4:(a + 1) * 4, :, :].rearrange("m p t -> p m t")), "xnld", 16)
    S.mark("xnT_ld", "xnld")
    S.inc(sy.dma_start(out=s2all[:, :].rearrange("p (db c) -> p db c", c=172), in_=s2q.rearrange("(db p c) -> p db c", p=128, c=172)), "xnld", 16)
    S.mark("s2_ld", "xnld")

    w13 = [(w1q, s1b), (w3q, s3b)]
    sqs_ = {0: s1q, 1: s3q}
    for pi, hts in enumerate(PASS_HT):
        hts = list(hts)
        nht = len(hts)
        for hi, ht in enumerate(hts):
            S.wmark_if(sy, "sb_cons")
            S.inc(sy.dma_start(out=s1b[:, :], in_=s1q.rearrange("(h c) -> h c", c=GRP)[128 * ht:128 * (ht + 1), :]), "sbld", 16)
            S.inc(sy.dma_start(out=s3b[:, :], in_=s3q.rearrange("(h c) -> h c", c=GRP)[128 * ht:128 * (ht + 1), :]), "sbld", 16)
            S.mark("sb_ld", "sbld")
            for wi, (wq, sb_) in enumerate(w13):
                for gq in range(4):
                    slot = gq % 2
                    pw = pw_i[slot]
                    pws = pw_s1
                    dqb = dq_big[slot]
                    wt = w1t[wi * 2 + slot]
                    S.wmark_if(gp, "pw_free_%d" % slot)
                    S.inc(gp.dma_start(out=pw[:, :],
                                       in_=wq.rearrange("(h k) c -> h k c", k=32)[128 * ht:128 * (ht + 1), 8 * gq:8 * gq + 8, :]), "pwx%d" % slot, 16)
                    S.mark("pwld", "pwx%d" % slot)
                    S.wmark(ve, "pwld")
                    S.wmark(ve, "sb_ld")
                    S.wmark_if(ve, "dq_free_%d" % slot)
                    dqv = dqb[:].rearrange("p (k h) -> p k h", h=128)
                    pwv = pws[:].rearrange("p (k c) -> p k c", c=GRP)
                    smv = sb_[:, 16 * gq:16 * gq + 16:2].rearrange("p (k o) -> p k o", o=1).broadcast_to([128, 8, GRP])
                    slv = sb_[:, 16 * gq + 1:16 * gq + 16:2].rearrange("p (k o) -> p k o", o=1).broadcast_to([128, 8, GRP])
                    S.wait(ve, "dv")
                    S.inc(ve.tensor_scalar(pws[:, :], pw[:, :], 4, None, ALU.arith_shift_right), "dv")
                    S.wait(ve, "dv")
                    S.inc(ve.tensor_tensor(dqv[:, :, 0:GRP], pwv, smv, ALU.mult), "dv")
                    S.wait(ve, "dv")
                    S.inc(ve.tensor_scalar(pws[:, :], pw[:, :], 15, 8, ALU.bitwise_and, ALU.bitwise_xor), "dv")
                    S.mark("pw_free_%d" % slot, "dv")
                    S.wait(ve, "dv")
                    S.inc(ve.scalar_tensor_tensor(dqv[:, :, GRP:128], pwv, 8.0, slv, ALU.subtract, ALU.mult), "dv")
                    S.mark("dq_%d" % slot, "dv")
                    S.wmark(sc, "dq_%d" % slot)
                    S.wmark_if(sc, "wt_cons_%d" % (wi * 2 + slot))
                    S.inc(sc.dma_start_transpose(wt[:].rearrange("p (g e) -> p g e", g=8), dqb[:].rearrange("p (g e) -> p g e", g=8)), "tdq", 16)
                    S.mark("dq_free_%d" % slot, "tdq")
                    S.mark("wtT_%d" % (wi * 2 + slot), "tdq")
                    pa = P[wi * 2]
                    pb = P[wi * 2 + 1]
                    S.wmark(te, "wtT_%d" % (wi * 2 + slot))
                    if gq == 0:
                        S.wmark(te, "xnT_ld" if (pi == 0 and hi == 0) else "f1ev_%d" % wi)
                    for tc in range(2):
                        pp = pa if tc == 0 else pb
                        for kk in range(8):
                            kd = 8 * gq + kk
                            ins = te.matmul(pp[:, 0:512],
                                            wt[:, kk * 128:(kk + 1) * 128],
                                            xnT[:, kd * T + tc * 512:kd * T + (tc + 1) * 512],
                                            start=(kd == 0), stop=(kd == NKD - 1))
                    S.inc(ins, "pe")
                    S.mark("wt_cons_%d" % (wi * 2 + slot), "pe")
                S.mark("f1mm_%d" % wi, "pe")
            S.mark("sb_cons", "dv")
            for tc in range(2):
                S.wmark(sc, "f1mm_0")
                S.wmark(sc, "f1mm_1")
                S.wmark_if(sc, "siluT_free")
                S.wait(sc, "ac")
                S.inc(sc.activation(silu_t[:, :], P[0 if tc == 0 else 1][:, 0:512], AF.Silu), "ac")
                S.wait(ve, "ac")
                S.wait(ve, "dv")
                S.inc(ve.tensor_tensor(hact[:, hi * T + tc * 512:hi * T + (tc + 1) * 512], silu_t[:, :],
                                       P[2 if tc == 0 else 3][:, 0:512], ALU.mult), "dv")
                S.mark("siluT_free", "dv")
            S.mark("f1ev_0", "dv")
            S.mark("f1ev_1", "dv")
        S.mark("ffn1_p%d" % pi, "dv")
        if pi == 0 and "hact" in dbg:
            d = dbg_t("hact", [128, 29 * T], BF16)
            S.wmark(sy, "ffn1_p0")
            S.inc(sy.dma_start(out=d, in_=hact[:, :]), "sp", 16)
            S.wait(sy, "sp")

        for dc in range(8):
            for dbl in range(4):
                db = 4 * dc + dbl
                slot = dbl % 2
                pw = pw_i[slot]
                pws = pw_s1
                dqb = dq_big[slot]
                ng2 = (nht + 7) // 8
                for g2q in range(ng2):
                    k20 = hts[0] + 8 * g2q
                    nk2 = min(8, hts[0] + nht - k20)
                    S.wmark_if(gp, "pw_free_%d" % slot)
                    S.inc(gp.dma_start(out=pw[:, 0:nk2 * GRP],
                                       in_=w2q.rearrange("(d k) c -> d k c", k=HID // 128)[128 * db:128 * (db + 1), k20:k20 + nk2, :]), "pwx%d" % slot, 16)
                    S.mark("pwld2", "pwx%d" % slot)
                    S.wmark(ve, "pwld2")
                    if pi == 0 and dc == 0 and dbl == 0 and g2q == 0:
                        S.wmark(ve, "s2_ld")
                    S.wmark_if(ve, "dq_free_%d" % slot)
                    dqv = dqb[:, 0:nk2 * 128].rearrange("p (k h) -> p k h", h=128)
                    pwv = pws[:, 0:nk2 * GRP].rearrange("p (k c) -> p k c", c=GRP)
                    smv = s2all[:, db * 172 + 2 * k20:db * 172 + 2 * (k20 + nk2):2].rearrange("p (k o) -> p k o", o=1).broadcast_to([128, nk2, GRP])
                    slv = s2all[:, db * 172 + 2 * k20 + 1:db * 172 + 2 * (k20 + nk2):2].rearrange("p (k o) -> p k o", o=1).broadcast_to([128, nk2, GRP])
                    S.wait(ve, "dv")
                    S.inc(ve.tensor_scalar(pws[:, 0:nk2 * GRP], pw[:, 0:nk2 * GRP], 4, None, ALU.arith_shift_right), "dv")
                    S.wait(ve, "dv")
                    S.inc(ve.tensor_tensor(dqv[:, :, 0:GRP], pwv, smv, ALU.mult), "dv")
                    S.wait(ve, "dv")
                    S.inc(ve.tensor_scalar(pws[:, 0:nk2 * GRP], pw[:, 0:nk2 * GRP], 15, 8, ALU.bitwise_and, ALU.bitwise_xor), "dv")
                    S.mark("pw_free_%d" % slot, "dv")
                    S.wait(ve, "dv")
                    S.inc(ve.scalar_tensor_tensor(dqv[:, :, GRP:128], pwv, 8.0, slv, ALU.subtract, ALU.mult), "dv")
                    S.mark("dq2_%d" % slot, "dv")
                    S.wmark(sc, "dq2_%d" % slot)
                    S.wmark_if(sc, "w2tmp_cons")
                    S.inc(sc.dma_start_transpose(dq_big[1 - slot][:, 0:nk2 * 128].rearrange("p (g e) -> p g e", g=nk2), dqb[:, 0:nk2 * 128].rearrange("p (g e) -> p g e", g=nk2)), "tdq", 16)
                    S.mark("w2T_ps", "tdq")
                    S.wmark(ve, "w2T_ps")
                    if pi > 0 or dc > 0:
                        S.wmark_if(ve, "gmm_done")
                    for kk in range(nk2):
                        k2l = k20 - hts[0] + kk
                        S.wait(ve, "dv")
                        S.inc(ve.tensor_copy(w2t_dc[:, k2l * 512 + dbl * 128:k2l * 512 + (dbl + 1) * 128],
                                             dq_big[1 - slot][:, kk * 128:(kk + 1) * 128]), "dv")
                    S.mark("w2tmp_cons", "dv")
                    S.mark("dq_free_%d" % slot, "dv")
                    S.mark("dq_free_%d" % (1 - slot), "dv")
            S.mark("w2dc_%d" % dc, "dv")
            if pi == 0 and dc == 0 and "w2t" in dbg:
                d = dbg_t("w2t", [128, 29 * 512], BF16)
                S.wmark(sy, "w2dc_0")
                S.inc(sy.dma_start(out=d, in_=w2t_dc[:, :]), "sp", 16)
                S.wait(sy, "sp")
            for ti in range(NQ):
                pg = P[4 + ti % 2]
                S.wmark(te, "w2dc_%d" % dc)
                S.wmark(te, "ffn1_p%d" % pi)
                S.wmark_if(te, "gev_%d" % (ti % 2))
                if pi == 0 and dc == 0 and ti <= 1:
                    S.wmark_if(te, "sev_0")
                    S.wmark_if(te, "sev_1")
                for hi in range(nht):
                    ins = te.matmul(pg[:, 0:512], hact[:, hi * T + ti * 128:hi * T + (ti + 1) * 128],
                                    w2t_dc[:, hi * 512:(hi + 1) * 512],
                                    start=(hi == 0), stop=(hi == nht - 1))
                S.inc(ins, "pe")
                S.mark("gmm_%d" % (ti % 2), "pe")
                if pi == 0:
                    S.wmark(ve, "gmm_%d" % (ti % 2))
                    S.wmark_if(ve, "gspill")
                    S.wait(ve, "dv")
                    S.inc(ve.tensor_copy(gin[:, :], pg[:, 0:512]), "dv")
                    S.mark("gev_%d" % (ti % 2), "dv")
                    S.wait(sy, "dv")
                    S.inc(sy.dma_start(out=g1_sp[ti, :, dc * 512:(dc + 1) * 512], in_=gin[:, :]), "sp", 16)
                    S.mark("gspill", "sp")
                    S.mark("gin_free", "sp")
                elif pi == 1:
                    S.wmark_if(sy, "gin_free")
                    if dc == 0 and ti == 0:
                        S.wait(sy, "sp")
                    S.inc(sy.dma_start(out=gin[:, :], in_=g1_sp[ti, :, dc * 512:(dc + 1) * 512]), "ginld", 16)
                    S.wmark(ve, "gmm_%d" % (ti % 2))
                    S.wait(ve, "ginld")
                    S.wmark_if(ve, "gspill")
                    S.wait(ve, "dv")
                    S.inc(ve.tensor_tensor(gin[:, :], gin[:, :], pg[:, 0:512], ALU.add), "dv")
                    S.mark("gev_%d" % (ti % 2), "dv")
                    S.mark("gin_free", "dv")
                    S.wait(sy, "dv")
                    S.inc(sy.dma_start(out=g2_sp[ti, :, dc * 512:(dc + 1) * 512], in_=gin[:, :]), "sp", 16)
                    S.mark("gspill", "sp")
                    S.mark("gin_free", "sp")
                else:
                    S.wmark_if(sy, "gin_free")
                    S.wmark_if(sy, "fin_done")
                    if dc == 0 and ti == 0:
                        S.wait(sy, "sp")
                        S.wmark(sy, "s_done")
                    S.inc(sy.dma_start(out=gin[:, :], in_=g2_sp[ti, :, dc * 512:(dc + 1) * 512]), "ginld", 16)
                    S.inc(sy.dma_start(out=sstg2[:, :], in_=s_sp[ti, :, dc * 512:(dc + 1) * 512]), "ginld", 16)
                    S.inc(sy.dma_start(out=xstg2[:, :], in_=x_pad[ET + ti * 128:ET + (ti + 1) * 128, dc * 512:(dc + 1) * 512]), "ginld", 16)
                    S.wmark(ve, "gmm_%d" % (ti % 2))
                    S.wait(ve, "ginld")
                    S.wait(ve, "dv")
                    S.inc(ve.tensor_tensor(gin[:, :], gin[:, :], pg[:, 0:512], ALU.add), "dv")
                    S.mark("gev_%d" % (ti % 2), "dv")
                    S.mark("gin_free", "dv")
                    S.wait(sc, "dv")
                    S.inc(sc.activation(gin[:, :], gin[:, :], AF.Silu), "ac")
                    S.wait(ve, "ac")
                    S.wait(ve, "dv")
                    S.inc(ve.tensor_tensor(ostg[:, :], gin[:, :], sstg2[:, :], ALU.mult), "dv")
                    S.wait(ve, "dv")
                    S.inc(ve.tensor_tensor(ostg[:, :], ostg[:, :], xstg2[:, :], ALU.add), "dv")
                    S.wait(sy, "dv")
                    S.inc(sy.dma_start(out=og[ti * 128:(ti + 1) * 128, dc * 512:(dc + 1) * 512], in_=ostg[:, :]), "out", 16)
                    S.mark("fin_done", "out")
            S.mark("gmm_done", "pe")

    if "g1" in dbg:
        d = dbg_t("g1", [NQ, 128, DM])
        S.wait(sy, "sp")
        S.inc(sy.dma_start(out=d, in_=g1_sp), "sp", 16)
    if "gfin" in dbg:
        d = dbg_t("gfin", [NQ, 128, DM])
        S.wait(sy, "sp")
        S.inc(sy.dma_start(out=d, in_=g2_sp), "sp", 16)

    S.wait(sy, "out")
    return nc, dbg_aps


def _shard_inputs(inputs):
    x = np.ascontiguousarray(inputs["L0_attn_hidden_states"])
    rng = np.random.default_rng()
    nv = int(rng.integers(1, 2**31 - 1))
    in_maps = []
    for c in range(8):
        b, half = c // 2, c % 2
        xp = np.zeros((TT, DM), dtype=np.float32)
        if half == 0:
            xp[ET:] = x[b, 0:T]
        else:
            xp[0:ET] = x[b, T - ET:T]
            xp[ET:] = x[b, T:2 * T]
        in_maps.append({
            "x_pad": xp,
            "w_in": np.ascontiguousarray(inputs["L0_attn_in_proj_w"], dtype=np.float32),
            "conv_w": np.ascontiguousarray(inputs["L0_attn_conv_w"], dtype=np.float32),
            "a_log": np.ascontiguousarray(inputs["L0_attn_A_log"], dtype=np.float32),
            "dt_bias": np.ascontiguousarray(inputs["L0_attn_dt_bias"], dtype=np.float32),
            "d_vec": np.ascontiguousarray(inputs["L0_attn_D"], dtype=np.float32),
            "norm_w": np.ascontiguousarray(inputs["L0_attn_norm_w"], dtype=np.float32),
            "ln_w": np.ascontiguousarray(inputs["L0_attn_ln_w"], dtype=np.float32),
            "mup": np.ascontiguousarray(inputs["L0_attn_mup_vector"], dtype=np.float32),
            "w_out": np.ascontiguousarray(inputs["L0_attn_out_proj_w"], dtype=np.float32),
            "w1q": np.ascontiguousarray(inputs["L0_ffn_w1"], dtype=np.int32),
            "s1q": np.ascontiguousarray(inputs["L0_ffn_s1"], dtype=np.float32),
            "w2q": np.ascontiguousarray(inputs["L0_ffn_w2"], dtype=np.int32),
            "s2q": np.ascontiguousarray(inputs["L0_ffn_s2"], dtype=np.float32),
            "w3q": np.ascontiguousarray(inputs["L0_ffn_w3"], dtype=np.int32),
            "s3q": np.ascontiguousarray(inputs["L0_ffn_s3"], dtype=np.float32),
            "flagv": np.full((128, 1), float(half), dtype=np.float32),
            "nonce": np.full((1, 16), nv, dtype=np.int32),
        })
    return in_maps


def kernel(**inputs):
    nc, _ = build_nc()
    in_maps = _shard_inputs(inputs)
    res = run_bass_kernel_spmd(nc, in_maps, core_ids=list(range(8)))
    out = np.empty((4, 2 * T, DM), dtype=np.float32)
    for c in range(8):
        b, half = c // 2, c % 2
        out[b, half * T:(half + 1) * T, :] = res.results[c]["og"]
    return out
